# version 03 (final)
# baseline (speedup 1.0000x reference)
from contextlib import ExitStack
import numpy as np
import ml_dtypes
import concourse.bass as bass
import concourse.mybir as mybir
from concourse.bass_utils import run_bass_kernel_spmd

F32 = mybir.dt.float32
BF16 = mybir.dt.bfloat16
AF = mybir.ActivationFunctionType
ALU = mybir.AluOpType
AX = mybir.AxisListType
BF = ml_dtypes.bfloat16

N_CORES = 8
T = 4096
D = 1024
NT = T // 128
ND = D // 128
LNEPS = 1e-5


class Sems:
    def __init__(self):
        self.h = {}
        self.c = {}

    def add(self, name, handle):
        self.h[name] = handle
        self.c[name] = 0

    def inc(self, instr, name, n=1):
        instr.then_inc(self.h[name], n)
        self.c[name] += n
        return self.c[name]

    def wait(self, eng, name, val=None):
        eng.wait_ge(self.h[name], self.c[name] if val is None else val)


def make_identity(nc, identity):
    sq = identity.shape[0]
    nc.gpsimd.memset(identity, 0.0)
    nc.gpsimd.affine_select(
        out=identity, in_=identity, compare_op=ALU.not_equal,
        fill=1.0, base=0, pattern=[[-1, sq]], channel_multiplier=1)


def make_strict_upper_neg(nc, mask, neg=-50.0):
    sq = mask.shape[0]
    nc.gpsimd.memset(mask, 0.0)
    nc.gpsimd.affine_select(
        out=mask, in_=mask, compare_op=ALU.is_ge,
        fill=neg, base=0, pattern=[[-1, sq]], channel_multiplier=1)


def pipelined_mm(block, S, groups, prod="t", cons="v", serial=False):
    base_p = S.c[prod]
    base_c = S.c[cons]
    for it, (emit_mm, emit_evac) in enumerate(groups):
        buf = it % 2

        @getattr(block, {"t": "tensor", "v": "vector", "s": "scalar", "g": "gpsimd"}[prod])
        def _(eng, it=it, buf=buf, emit_mm=emit_mm):
            eng.wait_ge(S.h[cons], base_c + (it if serial else max(0, it - 1)))
            S.inc(emit_mm(eng, buf), prod)

        @getattr(block, {"t": "tensor", "v": "vector", "s": "scalar", "g": "gpsimd"}[cons])
        def _(eng, it=it, buf=buf, emit_evac=emit_evac):
            eng.wait_ge(S.h[prod], base_p + it + 1)
            S.inc(emit_evac(eng, buf), cons)



def build_phase_a():
    nc = bass.Bass("TRN2", target_bir_lowering=False)
    x_d = nc.dram_tensor("x", [T, D], F32, kind="ExternalInput").ap()
    wq_d = nc.dram_tensor("wq", [D, 128], F32, kind="ExternalInput").ap()
    wk_d = nc.dram_tensor("wk", [D, 128], F32, kind="ExternalInput").ap()
    wv_d = nc.dram_tensor("wv", [D, 128], F32, kind="ExternalInput").ap()
    wo_d = nc.dram_tensor("wo", [128, D], F32, kind="ExternalInput").ap()
    out_d = nc.dram_tensor("x1T", [D, T], BF16, kind="ExternalOutput").ap()

    with ExitStack() as ctx:
        E = ctx.enter_context
        xT = E(nc.sbuf_tensor("xT", [128, ND * T], BF16))
        xnat = E(nc.sbuf_tensor("xnat", [128, NT * D], BF16))
        ident = E(nc.sbuf_tensor("ident", [128, 128], BF16))
        cmask = E(nc.sbuf_tensor("cmask", [128, 128], F32))
        wq = E(nc.sbuf_tensor("wq_s", [128, ND * 128], BF16))
        wk = E(nc.sbuf_tensor("wk_s", [128, ND * 128], BF16))
        wv = E(nc.sbuf_tensor("wv_s", [128, ND * 128], BF16))
        wo = E(nc.sbuf_tensor("wo_s", [128, D], BF16))
        qT = E(nc.sbuf_tensor("qT", [128, T], BF16))
        kT = E(nc.sbuf_tensor("kT", [128, T], BF16))
        vN = E(nc.sbuf_tensor("vN", [128, NT * 128], BF16))
        sstage = E(nc.sbuf_tensor("sstage", [128, T], F32))
        pP = E(nc.sbuf_tensor("pP", [128, T], BF16))
        pPT = E(nc.sbuf_tensor("pPT", [128, T], BF16))
        mx = E(nc.sbuf_tensor("mx", [128, 1], F32))
        den = E(nc.sbuf_tensor("den", [128, 1], F32))
        rden = E(nc.sbuf_tensor("rden", [128, 1], F32))
        oT = E(nc.sbuf_tensor("oT", [128, 128], BF16))
        sq2 = E(nc.sbuf_tensor("sq2", [128, ND * 256], BF16))
        onescol = E(nc.sbuf_tensor("onescol", [128, 1], BF16))
        statrow = E(nc.sbuf_tensor("statrow", [1, 512], F32))
        mubc = E(nc.sbuf_tensor("mubc", [128, 128], F32))
        rstdbc = E(nc.sbuf_tensor("rstdbc", [128, 128], F32))
        lnscr = E(nc.sbuf_tensor("lnscr", [128, 128], F32))
        attob = E(nc.sbuf_tensor("attob", [128, ND * 128], BF16))
        onesrow = E(nc.sbuf_tensor("onesrow", [1, 128], BF16))
        muhi = E(nc.sbuf_tensor("muhi", [1, 256], BF16))
        mulo = E(nc.sbuf_tensor("mulo", [1, 256], F32))
        mulo16 = E(nc.sbuf_tensor("mulo16", [1, 256], BF16))
        epsc = E(nc.sbuf_tensor("epsc", [1, 1], F32))
        pS0 = E(nc.psum_tensor("pS0", [128, 512], F32))
        pS1 = E(nc.psum_tensor("pS1", [128, 512], F32))
        pT0 = E(nc.psum_tensor("pT0", [128, 1024], BF16))
        pBc = E(nc.psum_tensor("pBc", [128, 256], F32))
        pAV = E(nc.psum_tensor("pAV", [128, 128], F32))
        pWo = E(nc.psum_tensor("pWo", [128, 1024], F32))
        pStat = E(nc.psum_tensor("pStat", [1, 512], F32))
        _dma = E(nc.semaphore("dmal"))
        _t = E(nc.semaphore("tsem"))
        _v = E(nc.semaphore("vsem"))
        _s = E(nc.semaphore("ssem"))
        _g = E(nc.semaphore("gsem"))
        _st = E(nc.semaphore("stout"))
        block = E(nc.Block())
        S = Sems()
        for n, h in (("dma", _dma), ("t", _t), ("v", _v), ("s", _s), ("g", _g), ("st", _st)):
            S.add(n, h)

        @block.gpsimd
        def _(g):
            for j in range(ND):
                S.inc(g.dma_start(out=wq[:, 128 * j:128 * (j + 1)], in_=wq_d[128 * j:128 * (j + 1), :]), "dma", 16)
                S.inc(g.dma_start(out=wk[:, 128 * j:128 * (j + 1)], in_=wk_d[128 * j:128 * (j + 1), :]), "dma", 16)
                S.inc(g.dma_start(out=wv[:, 128 * j:128 * (j + 1)], in_=wv_d[128 * j:128 * (j + 1), :]), "dma", 16)
            S.inc(g.dma_start(out=wo[:, :], in_=wo_d), "dma", 16)
            S.inc(g.dma_start(out=xnat[:, :].rearrange("p (n d) -> p n d", d=D),
                              in_=x_d.rearrange("(n p) d -> p n d", p=128)), "dma", 16)
            make_identity(nc, ident[:, :])
            make_strict_upper_neg(nc, cmask[:, :])
            g.memset(onescol[:, :], 1.0)
            g.memset(onesrow[:, :], 1.0)
            S.inc(g.memset(epsc[:, :], LNEPS), "g")
        ld_done = S.c["dma"]
        const_done = S.c["g"]

        @block.scalar
        def _(s):
            s.wait_ge(S.h["dma"], ld_done)
            S.inc(s.activation(wq[:, :], wq[:, :], AF.Copy, scale=float(128 ** -0.5)), "s")
        wq_scaled = S.c["s"]

        first_tr = [True]

        def tr_group(i):
            def emit_mm(t, buf, i=i):
                if first_tr[0]:
                    t.wait_ge(S.h["dma"], ld_done)
                    t.wait_ge(S.h["g"], const_done)
                    first_tr[0] = False
                for j in range(ND):
                    mm = t.transpose(pT0[:, 128 * j:128 * (j + 1)],
                                     xnat[:, D * i + 128 * j: D * i + 128 * (j + 1)],
                                     ident[:, :])
                return mm

            def emit_evac(v, buf, i=i):
                for j in range(ND):
                    mv = v.tensor_copy(xT[:, j * T + 128 * i: j * T + 128 * (i + 1)],
                                       pT0[:, 128 * j:128 * (j + 1)])
                return mv
            return emit_mm, emit_evac

        pipelined_mm(block, S, [tr_group(i) for i in range(NT)], serial=True)

        first_proj = [True]

        def proj_group(wbuf, dst, c):
            def emit_mm(t, buf):
                if first_proj[0]:
                    S.wait(t, "v")
                    t.wait_ge(S.h["s"], wq_scaled)
                    first_proj[0] = False
                p = pS0 if buf == 0 else pS1
                for j in range(ND):
                    mm = t.matmul(p[:, :], wbuf[:, 128 * j:128 * (j + 1)],
                                  xT[:, j * T + 512 * c: j * T + 512 * (c + 1)],
                                  start=(j == 0), stop=(j == ND - 1))
                return mm

            def emit_evac(v, buf):
                p = pS0 if buf == 0 else pS1
                return v.tensor_copy(dst[:, 512 * c:512 * (c + 1)], p[:, :])
            return emit_mm, emit_evac

        def v_group(i):
            def emit_mm(t, buf):
                p = (pS0 if buf == 0 else pS1)
                for j in range(ND):
                    mm = t.matmul(p[:, 0:128], xT[:, j * T + 128 * i: j * T + 128 * (i + 1)],
                                  wv[:, 128 * j:128 * (j + 1)],
                                  start=(j == 0), stop=(j == ND - 1))
                return mm

            def emit_evac(v, buf):
                p = (pS0 if buf == 0 else pS1)
                return v.tensor_copy(vN[:, 128 * i:128 * (i + 1)], p[:, 0:128])
            return emit_mm, emit_evac

        groups = []
        for wbuf, dst in ((wq, qT), (wk, kT)):
            for c in range(T // 512):
                groups.append(proj_group(wbuf, dst, c))
        for i in range(NT):
            groups.append(v_group(i))
        pipelined_mm(block, S, groups)

        for qt in range(NT):
            L = 128 * (qt + 1)
            nch = (L + 511) // 512

            def s_group(c, qt=qt, L=L):
                lo = 512 * c
                hi = min(L, 512 * (c + 1))

                def emit_mm(t, buf, lo=lo, hi=hi, qt=qt):
                    p = pS0 if buf == 0 else pS1
                    return t.matmul(p[:, 0:hi - lo], qT[:, 128 * qt:128 * (qt + 1)],
                                    kT[:, lo:hi], start=True, stop=True)

                def emit_evac(v, buf, lo=lo, hi=hi):
                    p = pS0 if buf == 0 else pS1
                    return v.tensor_copy(sstage[:, lo:hi], p[:, 0:hi - lo])
                return emit_mm, emit_evac

            pipelined_mm(block, S, [s_group(c) for c in range(nch)])

            @block.vector
            def _(v, qt=qt, L=L):
                v.tensor_tensor(sstage[:, L - 128:L], sstage[:, L - 128:L], cmask[:, :], op=ALU.add)
                S.inc(v.tensor_reduce(mx[:, :], sstage[:, 0:L], axis=AX.X, op=ALU.max, negate=True), "v")

            @block.scalar
            def _(s, qt=qt, L=L):
                S.wait(s, "v")
                S.inc(s.activation(pP[:, 0:L], sstage[:, 0:L], AF.Exp, bias=mx[:, :]), "s")

            @block.vector
            def _(v, qt=qt, L=L):
                S.wait(v, "s")
                S.inc(v.tensor_reduce(den[:, :], pP[:, 0:L], axis=AX.X, op=ALU.add), "v")

            @block.scalar
            def _(s, qt=qt):
                S.wait(s, "v")
                s.activation(rden[:, :], den[:, :], AF.Ln)
                S.inc(s.activation(rden[:, :], rden[:, :], AF.Exp, scale=-1.0), "s")

            @block.vector
            def _(v, qt=qt, L=L):
                S.wait(v, "s")
                S.inc(v.tensor_scalar_mul(pP[:, 0:L], pP[:, 0:L], rden[:, :]), "v")

            ntile = qt + 1

            def pt_group(g0, g1, qt=qt):
                def emit_mm(t, buf, g0=g0, g1=g1):
                    if g0 == 0:
                        S.wait(t, "v")
                    for k in range(g0, g1):
                        mm = t.transpose(pT0[:, 128 * (k - g0):128 * (k - g0) + 128],
                                         pP[:, 128 * k:128 * (k + 1)], ident[:, :])
                    return mm

                def emit_evac(v, buf, g0=g0, g1=g1):
                    for k in range(g0, g1):
                        mv = v.tensor_copy(pPT[:, 128 * k:128 * (k + 1)],
                                           pT0[:, 128 * (k - g0):128 * (k - g0) + 128])
                    return mv
                return emit_mm, emit_evac

            ptg = [pt_group(g0, min(g0 + 8, ntile)) for g0 in range(0, ntile, 8)]
            pipelined_mm(block, S, ptg, serial=True)

            @block.tensor
            def _(t, qt=qt, ntile=ntile):
                S.wait(t, "v")
                for k in range(ntile):
                    mm = t.matmul(pAV[:, :], vN[:, 128 * k:128 * (k + 1)],
                                  pPT[:, 128 * k:128 * (k + 1)],
                                  start=(k == 0), stop=(k == ntile - 1))
                S.inc(mm, "t")

            @block.vector
            def _(v, qt=qt):
                S.wait(v, "t")
                S.inc(v.tensor_copy(oT[:, :], pAV[:, :]), "v")

            @block.tensor
            def _(t, qt=qt):
                S.wait(t, "v")
                for j in range(ND):
                    mm = t.matmul(pWo[:, 128 * j:128 * (j + 1)], wo[:, 128 * j:128 * (j + 1)],
                                  oT[:, :], start=True, stop=True)
                S.inc(mm, "t")

            @block.scalar
            def _(s, qt=qt):
                S.wait(s, "t")
                for j in range(ND):
                    ms = s.copy(attob[:, 128 * j:128 * (j + 1)], pWo[:, 128 * j:128 * (j + 1)])
                S.inc(ms, "s")

            @block.vector
            def _(v, qt=qt):
                S.wait(v, "s")
                for j in range(ND):
                    mv = v.tensor_tensor(sq2[:, 256 * j:256 * j + 128],
                                         attob[:, 128 * j:128 * (j + 1)],
                                         xT[:, j * T + 128 * qt: j * T + 128 * (qt + 1)],
                                         op=ALU.add)
                S.inc(mv, "v")

            @block.scalar
            def _(s, qt=qt):
                S.wait(s, "v")
                for j in range(ND):
                    ms = s.activation(sq2[:, 256 * j + 128:256 * j + 256],
                                      sq2[:, 256 * j:256 * j + 128], AF.Square)
                S.inc(ms, "s")

            @block.tensor
            def _(t, qt=qt):
                S.wait(t, "s")
                for j in range(ND):
                    mm = t.matmul(pStat[0:1, 0:256], onescol[:, :], sq2[:, 256 * j:256 * (j + 1)],
                                  start=(j == 0), stop=(j == ND - 1))
                S.inc(mm, "t")

            @block.scalar
            def _(s, qt=qt):
                S.wait(s, "t")
                s.activation(statrow[0:1, 0:128], pStat[0:1, 0:128], AF.Copy, scale=1.0 / D)
                S.inc(s.activation(statrow[0:1, 128:256], pStat[0:1, 128:256], AF.Copy, scale=1.0 / D), "s")

            @block.vector
            def _(v, qt=qt):
                S.wait(v, "s")
                v.tensor_tensor(statrow[0:1, 256:384], statrow[0:1, 0:128], statrow[0:1, 0:128], op=ALU.mult)
                S.inc(v.tensor_tensor(statrow[0:1, 384:512], statrow[0:1, 128:256], statrow[0:1, 256:384], op=ALU.subtract), "v")

            @block.scalar
            def _(s, qt=qt):
                S.wait(s, "v")
                s.activation(statrow[0:1, 384:512], statrow[0:1, 384:512], AF.Ln, bias=epsc[:, :])
                S.inc(s.activation(statrow[0:1, 384:512], statrow[0:1, 384:512], AF.Exp, scale=-0.5), "s")

            @block.vector
            def _(v, qt=qt):
                S.wait(v, "s")
                v.tensor_copy(muhi[0:1, 0:128], statrow[0:1, 0:128])
                v.tensor_copy(muhi[0:1, 128:256], statrow[0:1, 384:512])
                v.tensor_tensor(mulo[0:1, 0:128], statrow[0:1, 0:128], muhi[0:1, 0:128], op=ALU.subtract)
                v.tensor_tensor(mulo[0:1, 128:256], statrow[0:1, 384:512], muhi[0:1, 128:256], op=ALU.subtract)
                S.inc(v.tensor_copy(mulo16[0:1, 0:256], mulo[0:1, 0:256]), "v")

            @block.tensor
            def _(t, qt=qt):
                S.wait(t, "v")
                t.matmul(pBc[:, 0:256], onesrow[0:1, 0:128], muhi[0:1, 0:256], start=True, stop=False)
                S.inc(t.matmul(pBc[:, 0:256], onesrow[0:1, 0:128], mulo16[0:1, 0:256], start=False, stop=True), "t")

            @block.vector
            def _(v, qt=qt):
                S.wait(v, "t")
                v.tensor_copy(mubc[:, :], pBc[:, 0:128])
                S.inc(v.tensor_copy(rstdbc[:, :], pBc[:, 128:256]), "v")
                for j in range(ND):
                    v.tensor_tensor(lnscr[:, :], sq2[:, 256 * j:256 * j + 128], mubc[:, :], op=ALU.subtract)
                    mv = v.tensor_tensor(xT[:, j * T + 128 * qt: j * T + 128 * (qt + 1)],
                                         lnscr[:, :], rstdbc[:, :], op=ALU.mult)
                S.inc(mv, "v")

        @block.gpsimd
        def _(g):
            S.wait(g, "v")
            for j in range(ND):
                S.inc(g.dma_start(out=out_d[128 * j:128 * (j + 1), :], in_=xT[:, j * T:(j + 1) * T]), "st", 16)
            S.wait(g, "st")

    return nc


def _prep_cols(v):
    return np.ascontiguousarray(v.reshape(-1, 128).T.astype(np.float32))


def kernel(**inputs):
    g = inputs
    BF = ml_dtypes.bfloat16
    B = 8

    ncA = build_phase_a()
    x = g["L0_attn_x"]
    inA = [dict(x=np.ascontiguousarray(x[b]), wq=g["L0_attn_Wq"], wk=g["L0_attn_Wk"],
                wv=g["L0_attn_Wv"], wo=g["L0_attn_Wo"]) for b in range(B)]
    resA = run_bass_kernel_spmd(ncA, inA, core_ids=list(range(B)))
    xa = [np.ascontiguousarray(resA.results[b]["x1T"]) for b in range(B)]

    qsr = np.zeros((6, 128, 4), np.float32)
    for l in range(6):
        qs = g["L0_ffn_q_scale"][l]; ks = g["L0_ffn_k_scale"][l]
        sw = np.concatenate([qs[32:], qs[:32]])
        ksw = np.concatenate([ks[32:], ks[:32]])
        qsr[l, :, 0] = np.tile(qs, 2); qsr[l, :, 1] = np.tile(sw, 2)
        qsr[l, :, 2] = np.tile(ks, 2); qsr[l, :, 3] = np.tile(ksw, 2)
    b1r = np.zeros((6, 128, 44), np.float32)
    for l in range(6):
        b1 = g["L0_ffn_b1"][l]
        for m in range(21):
            b1r[l, :, m] = b1[128 * m:128 * (m + 1)]
        b1r[l, 0:42, 21] = b1[2688:2730]
        for m in range(21):
            b1r[l, :, 22 + m] = b1[2730 + 128 * m:2730 + 128 * (m + 1)]
        b1r[l, 0:42, 43] = b1[5418:5460]
    b2r = np.stack([_prep_cols(g["L0_ffn_b2"][l]) for l in range(6)])
    lngr = np.stack([_prep_cols(g["L0_ffn_ln_g"][l]) for l in range(6)])
    lnbr = np.stack([_prep_cols(g["L0_ffn_ln_b"][l]) for l in range(6)])
    ncB = build_phase_b(0, 6, do_final=True)
    commonB = dict(wqkv=g["L0_ffn_Wqkv"], wo=g["L0_ffn_Wo"], w1=g["L0_ffn_W1"],
                   w2=g["L0_ffn_W2"], qsr=qsr, b1r=b1r, b2r=b2r, lngr=lngr, lnbr=lnbr,
                   **phase_b_consts())
    inB = [dict(xbin=xa[b], **commonB) for b in range(B)]
    resB = run_bass_kernel_spmd(ncB, inB, core_ids=list(range(B)))
    xb = [resB.results[b]["x2T"] for b in range(B)]

    tW = lambda w: np.ascontiguousarray(w.T.astype(np.float32))
    commonC = dict(wkT=tW(g["L1_attn_Wk"]), wvT=tW(g["L1_attn_Wv"]),
                   wrT=tW(g["L1_attn_Wr"]), woT=tW(g["L1_attn_W_out"]),
                   mukc=_prep_cols(g["L1_attn_mu_k"]), muvc=_prep_cols(g["L1_attn_mu_v"]),
                   murc=_prep_cols(g["L1_attn_mu_r"]), wcc=_prep_cols(g["L1_attn_w"]),
                   ucc=_prep_cols(g["L1_attn_u"]))
    ncC = build_phase_c()
    inC = []
    for s in range(B):
        sl = np.concatenate([xb[b][:, 512 * s:512 * (s + 1)] for b in range(B)], axis=1)
        inC.append(dict(xcin=np.ascontiguousarray(sl), **commonC))
    resC = run_bass_kernel_spmd(ncC, inC, core_ids=list(range(B)))
    xcs = [resC.results[s]["xcT"] for s in range(B)]

    ncD = build_phase_d(4)
    commonD = dict(qw=g["L1_ffn_qw"], kw=g["L1_ffn_kw"], vw=g["L1_ffn_vw"],
                   ow=g["L1_ffn_ow"], qbc=_prep_cols(g["L1_ffn_qb"]),
                   kbc=_prep_cols(g["L1_ffn_kb"]),
                   vbr=np.ascontiguousarray(g["L1_ffn_vb"][None, :].astype(np.float32)),
                   obc=_prep_cols(g["L1_ffn_ob"]), gatew=g["L1_ffn_gate_w"],
                   w1e=g["L1_ffn_w1"], w2e=g["L1_ffn_w2"],
                   n1c=_prep_cols(g["L1_ffn_norm1_w"]), n2c=_prep_cols(g["L1_ffn_norm2_w"]))
    inD = []
    for b in range(B):
        xd_in = np.concatenate([xcs[s][:, 512 * b:512 * (b + 1)] for s in range(B)], axis=1)
        cm = np.zeros((8, 8), np.float32); cm[:b, :] = 1.0
        inD.append(dict(xdin=np.ascontiguousarray(xd_in), coremask=cm, **commonD))
    resD = run_bass_kernel_spmd(ncD, inD, core_ids=list(range(B)))
    out = np.stack([resD.results[b]["xdT"].astype(np.float32).T for b in range(B)])
    return out




_ENG = {"t": "tensor", "v": "vector", "s": "scalar", "g": "gpsimd"}


def pipelineN(block, S, nc_sems, groups, stages, depth=2):
    nst = len(stages)
    names = []
    for k, st in enumerate(stages):
        nm = f"pl{k}"
        if nm not in S.h:
            S.add(nm, nc_sems[k])
        names.append(nm)
    base = [S.c[n] for n in names]
    S.entry_base = {n: S.c[n] for n in S.h if n.startswith("pl")}
    incs = [st.get("inc", 1) for st in stages]
    for gi in range(len(groups)):
        buf = gi % depth
        for k, st in enumerate(stages):
            eng_name = _ENG[st["eng"]]

            @getattr(block, eng_name)
            def _(eng, gi=gi, buf=buf, k=k, st=st):
                if k == 0:
                    if gi < depth:
                        for nm2 in S.h:
                            if nm2.startswith("pl"):
                                eng.wait_ge(S.h[nm2], S.entry_base.get(nm2, 0))
                    else:
                        eng.wait_ge(S.h[names[nst - 1]], base[nst - 1] + (gi - depth + 1) * incs[nst - 1])
                else:
                    eng.wait_ge(S.h[names[k - 1]], base[k - 1] + (gi + 1) * incs[k - 1])
                for (si, lag) in st.get("extra", ()):
                    eng.wait_ge(S.h[names[si]], base[si] + max(0, (gi - lag + 1)) * incs[si])
                instr = st["fn"](eng, buf, gi)
                S.inc(instr, names[k], incs[k])



HEADS = 16
DH = 64
INNER = 1024
DIM_INNER = 2730
NI = 22
W1C = 5460
QK_SCALE = 8.0
CH = 256
MC = 512
LY = 3 * INNER


def build_phase_b(lay_lo=0, lay_hi=6, do_final=None):
    if do_final is None:
        do_final = (lay_hi == 6)
    nc = bass.Bass("TRN2", target_bir_lowering=False)
    xin_d = nc.dram_tensor("xbin", [D, T], BF16, kind="ExternalInput").ap()
    wqkv_d = nc.dram_tensor("wqkv", [6, D, LY], F32, kind="ExternalInput").ap()
    wo_d = nc.dram_tensor("wo", [6, INNER, D], F32, kind="ExternalInput").ap()
    w1_d = nc.dram_tensor("w1", [6, D, W1C], F32, kind="ExternalInput").ap()
    w2_d = nc.dram_tensor("w2", [6, DIM_INNER, D], F32, kind="ExternalInput").ap()
    qsr_d = nc.dram_tensor("qsr", [6, 128, 4], F32, kind="ExternalInput").ap()
    b1r_d = nc.dram_tensor("b1r", [6, 128, 44], F32, kind="ExternalInput").ap()
    b2r_d = nc.dram_tensor("b2r", [6, 128, 8], F32, kind="ExternalInput").ap()
    lngr_d = nc.dram_tensor("lngr", [6, 128, 8], F32, kind="ExternalInput").ap()
    lnbr_d = nc.dram_tensor("lnbr", [6, 128, 8], F32, kind="ExternalInput").ap()
    csq_d = nc.dram_tensor("csq", [128, T], F32, kind="ExternalInput").ap()
    snq_d = nc.dram_tensor("snq", [128, T], F32, kind="ExternalInput").ap()
    csk_d = nc.dram_tensor("csk", [128, T], F32, kind="ExternalInput").ap()
    snk_d = nc.dram_tensor("snk", [128, T], F32, kind="ExternalInput").ap()
    blk16_d = nc.dram_tensor("blk16", [128, 8 * 16], F32, kind="ExternalInput").ap()
    sel16_d = nc.dram_tensor("sel16", [16, 8 * 128], F32, kind="ExternalInput").ap()
    perm_d = nc.dram_tensor("perm", [128, 128], F32, kind="ExternalInput").ap()
    out_d = nc.dram_tensor("x2T", [D, T], BF16, kind="ExternalOutput").ap()

    with ExitStack() as ctx:
        E = ctx.enter_context
        xT = E(nc.sbuf_tensor("xT", [128, ND * T], BF16))
        regW = E(nc.sbuf_tensor("regW", [128, ND * LY], BF16))
        pool = E(nc.sbuf_tensor("pool", [128, 20480], BF16))
        kring = pool[:, 0:2048]
        vring = pool[:, 2048:4096]
        qrot = pool[:, 4096:5120]
        rawb = pool[:, 5120:7168]
        oTb_p = pool[:, 5120:7168]
        woS = pool[:, 7168:15360]
        tblp = pool[:, 15360:16384]
        xsst = pool[:, 16384:17408]
        AH = pool[:, 0:11264]
        xsn = pool[:, 11264:15360]
        sqg = pool[:, 15360:17408]
        gsc = pool[:, 17408:18432]
        w1tp = pool[:, 18432:20480]
        xinb = pool[:, 0:4096]
        raise_pool_layout = None
        sqb = E(nc.sbuf_tensor("sqb", [128, 2 * 256], BF16))
        ssf = E(nc.sbuf_tensor("ssf", [16, 256], F32))
        inv16 = E(nc.sbuf_tensor("inv16", [16, 256], BF16))
        invb = E(nc.sbuf_tensor("invb", [128, 2 * 256], BF16))
        permb = E(nc.sbuf_tensor("permb", [128, 2 * 256], BF16))
        rot1 = E(nc.sbuf_tensor("rot1", [128, 2 * 256], BF16))
        rot2 = sqb
        stash = E(nc.sbuf_tensor("stash", [128, 8], BF16))
        blk16 = E(nc.sbuf_tensor("blk16s", [128, 8 * 16], BF16))
        sel16 = E(nc.sbuf_tensor("sel16s", [16, 8 * 128], BF16))
        perm = E(nc.sbuf_tensor("perms", [128, 128], BF16))
        ident = E(nc.sbuf_tensor("identb", [128, 128], BF16))
        qscol = E(nc.sbuf_tensor("qscol", [128, 4], F32))
        ln8c = E(nc.sbuf_tensor("ln8c", [16, 2], F32))
        m0 = E(nc.sbuf_tensor("m0", [128, 128], F32))
        m1 = E(nc.sbuf_tensor("m1", [128, 256], F32))
        mband = E(nc.sbuf_tensor("mband", [128, 384], F32))
        sst = E(nc.sbuf_tensor("sstb", [128, 2 * 384], F32))
        pP = E(nc.sbuf_tensor("pPb", [128, 2 * 384], BF16))
        pPT = E(nc.sbuf_tensor("pPTb", [128, 2 * 384], BF16))
        mx2 = E(nc.sbuf_tensor("mx2", [128, 2], F32))
        den2 = E(nc.sbuf_tensor("den2", [128, 2], F32))
        rden2 = E(nc.sbuf_tensor("rden2", [128, 2], F32))
        attob = E(nc.sbuf_tensor("attobb", [128, 2 * 512], BF16))
        b1c = E(nc.sbuf_tensor("b1c", [128, 44], F32))
        b2c = E(nc.sbuf_tensor("b2c", [128, 8], F32))
        gcol = E(nc.sbuf_tensor("gcol", [128, 8], F32))
        bcol = E(nc.sbuf_tensor("bcol", [128, 8], F32))
        ffnob = E(nc.sbuf_tensor("ffnob", [128, 2 * 512], BF16))
        onescol = E(nc.sbuf_tensor("onescol", [128, 1], BF16))
        onesrow = E(nc.sbuf_tensor("onesrow", [1, 128], BF16))
        statrow = E(nc.sbuf_tensor("statrow", [1, 1536], F32))
        muhi = E(nc.sbuf_tensor("muhi", [1, 512], BF16))
        mulo = E(nc.sbuf_tensor("mulo", [1, 512], F32))
        mulo16 = E(nc.sbuf_tensor("mulo16", [1, 512], BF16))
        mubcf = E(nc.sbuf_tensor("mubcf", [128, 2 * 256], F32))
        rstdf = E(nc.sbuf_tensor("rstdf", [128, 2 * 256], F32))
        lnscr = E(nc.sbuf_tensor("lnscr", [128, 256], F32))
        epsc = E(nc.sbuf_tensor("epsc", [1, 1], F32))
        pA = E(nc.psum_tensor("pA", [128, 512], F32))
        pB = E(nc.psum_tensor("pB", [128, 512], F32))
        pC = E(nc.psum_tensor("pC", [128, 512], F32))
        pD = E(nc.psum_tensor("pD", [128, 512], F32))
        pT0 = E(nc.psum_tensor("pT0b", [128, 1024], BF16))
        pSS = E(nc.psum_tensor("pSS", [128, 512], F32))
        pStat = E(nc.psum_tensor("pStatb", [1, 512], F32))
        pBc = E(nc.psum_tensor("pBcb", [1, 512], F32) if False else nc.psum_tensor("pBcb", [128, 512], F32))
        sems = {}
        for n in ("dma", "dmb", "t", "v", "s", "g", "st"):
            sems[n] = E(nc.semaphore(n))
        plsems = [E(nc.semaphore(f"pls{k}")) for k in range(10)]
        block = E(nc.Block())
        S = Sems()
        for n, h in sems.items():
            S.add(n, h)
        PAIRS = [(pA, pB), (pC, pD)]
        NCH = T // CH
        NQC = CH // 128
        NMC = T // MC
        NLG = MC // 256

        kring = pool[:, 0:4096]
        vring = pool[:, 4096:8192]
        qrot = pool[:, 8192:10240]
        rawb = pool[:, 10240:12288]
        oTb = pool[:, 12288:14336]
        tblp = pool[:, 14336:15360]
        xsst = pool[:, 15360:16384]
        AH = pool[:, 0:11264]
        xsn = pool[:, 11264:15360]
        sqg = pool[:, 15360:17408]
        gsc = pool[:, 17408:18432]
        w1tp = pool[:, 18432:20480]
        xinb = pool[:, 0:4096]
        woS = E(nc.sbuf_tensor("woSb", [128, ND * 1024], BF16))

        @block.gpsimd
        def _(g):
            for j in range(ND):
                S.inc(g.dma_start(out=xT[:, j * T:(j + 1) * T], in_=xin_d[128 * j:128 * (j + 1), :]), "dma", 16)
            S.inc(g.dma_start(out=blk16[:, :], in_=blk16_d), "dma", 16)
            S.inc(g.dma_start(out=sel16[:, :], in_=sel16_d), "dma", 16)
            S.inc(g.dma_start(out=perm[:, :], in_=perm_d), "dma", 16)
            make_identity(nc, ident[:, :])
            make_strict_upper_neg(nc, m0[:, :])
            g.memset(m1[:, :], 0.0)
            g.affine_select(out=m1[:, :], in_=m1[:, :], compare_op=ALU.is_ge,
                            fill=-50.0, base=128, pattern=[[-1, 256]], channel_multiplier=1)
            g.memset(mband[:, :], 0.0)
            g.affine_select(out=mband[:, :], in_=mband[:, :], compare_op=ALU.is_ge,
                            fill=-50.0, base=0, pattern=[[1, 384]], channel_multiplier=-1)
            g.affine_select(out=mband[:, :], in_=mband[:, :], compare_op=ALU.is_ge,
                            fill=-50.0, base=256, pattern=[[-1, 384]], channel_multiplier=1)
            g.memset(onescol[:, :], 1.0)
            g.memset(onesrow[:, :], 1.0)
            g.memset(epsc[:, :], LNEPS)
            g.memset(ln8c[:, 0:1], float(np.log(QK_SCALE)))
            S.inc(g.memset(ln8c[:, 1:2], 0.0), "g")
        entry_dma = S.c["dma"]
        entry_g = S.c["g"]
        for engn in ("tensor", "vector", "scalar"):
            @getattr(block, engn)
            def _(e):
                e.wait_ge(S.h["dma"], entry_dma)
                e.wait_ge(S.h["g"], entry_g)

        def wait_pl(eng):
            for nm in list(S.h):
                if nm.startswith("pl"):
                    eng.wait_ge(S.h[nm], S.c[nm])

        for l in range(lay_lo, lay_hi):
            @block.gpsimd
            def _(g, l=l):
                wait_pl(g)
                for j in range(ND):
                    S.inc(g.dma_start(out=regW[:, j * LY:(j + 1) * LY],
                                      in_=wqkv_d[l, 128 * j:128 * (j + 1), :]), "dma", 16)
                for j in range(ND):
                    S.inc(g.dma_start(out=woS[:, j * 1024:(j + 1) * 1024],
                                      in_=wo_d[l, 128 * j:128 * (j + 1), :]), "dma", 16)
                S.inc(g.dma_start(out=qscol[:, :], in_=qsr_d[l]), "dma", 16)
            wq_dma = S.c["dma"]

            for c in range(NCH):
                c0 = CH * c

                @block.gpsimd
                def _(g, c=c, c0=c0, wq_dma=wq_dma):
                    if c == 0:
                        g.wait_ge(S.h["dma"], wq_dma)
                    S.wait(g, "v")
                    S.inc(g.dma_start(out=tblp[:, 0:256], in_=csq_d[:, c0:c0 + CH]), "dmb", 16)
                    S.inc(g.dma_start(out=tblp[:, 256:512], in_=snq_d[:, c0:c0 + CH]), "dmb", 16)
                    S.inc(g.dma_start(out=tblp[:, 512:768], in_=csk_d[:, c0:c0 + CH]), "dmb", 16)
                    S.inc(g.dma_start(out=tblp[:, 768:1024], in_=snk_d[:, c0:c0 + CH]), "dmb", 16)

                @block.vector
                def _(v, c=c, c0=c0):
                    S.wait(v, "dmb")
                    for i in range(4):
                        v.tensor_scalar_mul(tblp[:, 256 * i:256 * (i + 1)],
                                            tblp[:, 256 * i:256 * (i + 1)], qscol[:, i:i + 1])
                    for j in range(4, 8):
                        if c == 0:
                            v.memset(xsst[:, (j - 4) * 256:(j - 4) * 256 + 1], 0.0)
                        else:
                            v.tensor_copy(xsst[:, (j - 4) * 256:(j - 4) * 256 + 1], stash[:, j - 4:j - 3])
                        mv = v.tensor_copy(xsst[:, (j - 4) * 256 + 1:(j - 4) * 256 + 256],
                                           xT[:, j * T + c0: j * T + c0 + 255])
                    S.inc(mv, "v")
                tbl_v = S.c["v"]

                def rhs_xs(j, lo, n, c0=c0):
                    if j < 4:
                        return xT[:, j * T + c0 + lo: j * T + c0 + lo + n]
                    return xsst[:, (j - 4) * 256 + lo:(j - 4) * 256 + lo + n]

                for which, dst_is in ((0, "q"), (1, "k")):
                    def pr_fn(eng, buf, gi, which=which, tbl_v=tbl_v):
                        if which == 0 and gi == 0:
                            eng.wait_ge(S.h["v"], tbl_v)
                        p = pA if buf == 0 else pB
                        for j in range(ND):
                            mm = eng.matmul(p[:, 0:CH],
                                            regW[:, j * LY + which * INNER + 128 * gi:
                                                 j * LY + which * INNER + 128 * (gi + 1)],
                                            rhs_xs(j, 0, CH), start=(j == 0), stop=(j == ND - 1))
                        return mm

                    def pr_ev(eng, buf, gi):
                        p = pA if buf == 0 else pB
                        return eng.copy(rawb[:, CH * gi:CH * (gi + 1)], p[:, 0:CH])

                    pipelineN(block, S, plsems, list(range(ND)),
                              [dict(eng="t", fn=pr_fn), dict(eng="s", fn=pr_ev)])

                    def sq_fn(eng, buf, gi):
                        return eng.tensor_tensor(sqb[:, CH * buf:CH * (buf + 1)],
                                                 rawb[:, CH * gi:CH * (gi + 1)],
                                                 rawb[:, CH * gi:CH * (gi + 1)], op=ALU.mult)

                    def ssmm_fn(eng, buf, gi):
                        return eng.matmul(pSS[0:16, 0:CH], blk16[:, 16 * gi:16 * (gi + 1)],
                                          sqb[:, CH * buf:CH * (buf + 1)],
                                          start=(gi == 0), stop=(gi == ND - 1))

                    pipelineN(block, S, plsems, list(range(ND)),
                              [dict(eng="v", fn=sq_fn), dict(eng="t", fn=ssmm_fn)])

                    @block.vector
                    def _(v):
                        S.wait(v, "pl1")
                        S.inc(v.tensor_scalar_max(ssf[:, :], pSS[0:16, 0:CH], 1e-24), "v")

                    @block.scalar
                    def _(s, which=which):
                        S.wait(s, "v")
                        s.activation(ssf[:, :], ssf[:, :], AF.Ln)
                        S.inc(s.activation(inv16[:, :], ssf[:, :], AF.Exp, scale=-0.5,
                                           bias=ln8c[:, which:which + 1]), "s")
                    inv_s = S.c["s"]

                    def ep_fn(eng, buf, gi, inv_s=inv_s):
                        if gi == 0:
                            eng.wait_ge(S.h["s"], inv_s)
                        pE = pA if buf == 0 else pC
                        pF = pB if buf == 0 else pD
                        eng.matmul(pE[:, 0:CH], sel16[:, 128 * gi:128 * (gi + 1)], inv16[:, :],
                                   start=True, stop=True)
                        return eng.matmul(pF[:, 0:CH], perm[:, :], rawb[:, CH * gi:CH * (gi + 1)],
                                          start=True, stop=True)

                    def ep_ev(eng, buf, gi):
                        pE = pA if buf == 0 else pC
                        pF = pB if buf == 0 else pD
                        eng.copy(invb[:, CH * buf:CH * (buf + 1)], pE[:, 0:CH])
                        return eng.copy(permb[:, CH * buf:CH * (buf + 1)], pF[:, 0:CH])

                    def rot_fn(eng, buf, gi, which=which, c=c):
                        cse = tblp[:, 512 * which: 512 * which + 256]
                        sne = tblp[:, 512 * which + 256: 512 * which + 512]
                        b5 = CH * buf
                        eng.tensor_tensor(rot1[:, b5:b5 + CH], rawb[:, CH * gi:CH * (gi + 1)], cse, op=ALU.mult)
                        eng.tensor_tensor(rot2[:, b5:b5 + CH], permb[:, b5:b5 + CH], sne, op=ALU.mult)
                        eng.tensor_tensor(rot1[:, b5:b5 + CH], rot1[:, b5:b5 + CH], rot2[:, b5:b5 + CH], op=ALU.add)
                        if which == 0:
                            dst = qrot[:, CH * gi:CH * (gi + 1)]
                        else:
                            dst = kring[:, gi * 512 + (NQC * c % 4) * 128: gi * 512 + (NQC * c % 4) * 128 + CH]
                        return eng.tensor_tensor(dst, rot1[:, b5:b5 + CH], invb[:, b5:b5 + CH], op=ALU.mult)

                    pipelineN(block, S, plsems, list(range(ND)),
                              [dict(eng="t", fn=ep_fn), dict(eng="s", fn=ep_ev),
                               dict(eng="v", fn=rot_fn)])

                def v_fn(eng, buf, gi):
                    tk, dv = divmod(gi, 2)
                    p = pA if buf == 0 else pB
                    for j in range(ND):
                        mm = eng.matmul(p[:, :], rhs_xs(j, 128 * tk, 128),
                                        regW[:, j * LY + 2 * INNER + 512 * dv:
                                             j * LY + 2 * INNER + 512 * (dv + 1)],
                                        start=(j == 0), stop=(j == ND - 1))
                    return mm

                def v_ev(eng, buf, gi, c=c):
                    tk, dv = divmod(gi, 2)
                    p = pA if buf == 0 else pB
                    slot = (NQC * c + tk) % 4
                    return eng.copy(vring[:, slot * 1024 + 512 * dv: slot * 1024 + 512 * (dv + 1)], p[:, :])

                pipelineN(block, S, plsems, list(range(2 * NQC)),
                          [dict(eng="t", fn=v_fn), dict(eng="s", fn=v_ev)])

                @block.vector
                def _(v, c0=c0):
                    S.wait(v, "pl2")
                    for j in range(4, 8):
                        mv = v.tensor_copy(stash[:, j - 4:j - 3], xT[:, j * T + c0 + 255: j * T + c0 + 256])
                    S.inc(mv, "v")

                def att_stage(k):
                    def fn(eng, buf, gi, c=c, k=k):
                        qti, h = divmod(gi, HEADS)
                        qtg = NQC * c + qti
                        Wd = min(384, 128 * (qtg + 1))
                        ktiles = list(range(max(0, qtg - 2), qtg + 1))
                        po = 64 * (h % 2)
                        jt = h // 2
                        pX = pC if buf == 0 else pD
                        b34 = 384 * buf
                        if k == 0:
                            for ki, kt in enumerate(ktiles):
                                mm = eng.matmul(
                                    pX[:, 128 * ki:128 * (ki + 1)],
                                    qrot[po:po + 64, jt * CH + 128 * qti: jt * CH + 128 * (qti + 1)],
                                    kring[po:po + 64, jt * 512 + (kt % 4) * 128: jt * 512 + (kt % 4) * 128 + 128],
                                    start=True, stop=True,
                                    tile_position=(po, 0) if po else None)
                            return mm
                        if k == 1:
                            msk = (m0 if qtg == 0 else m1 if qtg == 1 else mband)
                            eng.tensor_tensor(sst[:, b34:b34 + Wd], pX[:, 0:Wd], msk[:, 0:Wd], op=ALU.add)
                            return eng.tensor_reduce(mx2[:, buf:buf + 1], sst[:, b34:b34 + Wd],
                                                     axis=AX.X, op=ALU.max, negate=True)
                        if k == 2:
                            return eng.activation(pP[:, b34:b34 + Wd], sst[:, b34:b34 + Wd],
                                                  AF.Exp, bias=mx2[:, buf:buf + 1])
                        if k == 3:
                            return eng.tensor_reduce(den2[:, buf:buf + 1], pP[:, b34:b34 + Wd],
                                                     axis=AX.X, op=ALU.add)
                        if k == 4:
                            eng.activation(rden2[:, buf:buf + 1], den2[:, buf:buf + 1], AF.Ln)
                            return eng.activation(rden2[:, buf:buf + 1], rden2[:, buf:buf + 1],
                                                  AF.Exp, scale=-1.0)
                        if k == 5:
                            return eng.tensor_scalar_mul(pP[:, b34:b34 + Wd], pP[:, b34:b34 + Wd],
                                                         rden2[:, buf:buf + 1])
                        if k == 6:
                            for ki in range(len(ktiles)):
                                mm = eng.transpose(pT0[:, 128 * ki:128 * (ki + 1)],
                                                   pP[:, b34 + 128 * ki: b34 + 128 * (ki + 1)], ident[:, :])
                            return mm
                        if k == 7:
                            for ki in range(len(ktiles)):
                                mv = eng.tensor_copy(pPT[:, b34 + 128 * ki: b34 + 128 * (ki + 1)],
                                                     pT0[:, 128 * ki:128 * (ki + 1)])
                            return mv
                        if k == 8:
                            for ki, kt in enumerate(ktiles):
                                mm = eng.matmul(pX[0:64, 384:512],
                                                vring[:, (kt % 4) * 1024 + 64 * h: (kt % 4) * 1024 + 64 * (h + 1)],
                                                pPT[:, b34 + 128 * ki: b34 + 128 * (ki + 1)],
                                                start=(ki == 0), stop=(ki == len(ktiles) - 1))
                            return mm
                        if k == 9:
                            return eng.copy(oTb[po:po + 64, qti * 1024 + jt * 128: qti * 1024 + jt * 128 + 128],
                                            pX[0:64, 384:512])
                    return fn

                pipelineN(block, S, plsems, list(range(NQC * HEADS)),
                          [dict(eng="t", fn=att_stage(0)),
                           dict(eng="v", fn=att_stage(1)),
                           dict(eng="s", fn=att_stage(2)),
                           dict(eng="v", fn=att_stage(3)),
                           dict(eng="s", fn=att_stage(4)),
                           dict(eng="v", fn=att_stage(5)),
                           dict(eng="t", fn=att_stage(6), extra=[(7, 1)]),
                           dict(eng="v", fn=att_stage(7)),
                           dict(eng="t", fn=att_stage(8)),
                           dict(eng="s", fn=att_stage(9))])

                def wo_fn(eng, buf, gi):
                    qti, half = divmod(gi, 2)
                    pX, pY = PAIRS[buf]
                    p = pX if half == 0 else pY
                    for jj in range(4):
                        j = 4 * half + jj
                        for it in range(ND):
                            mm = eng.matmul(p[:, 128 * jj:128 * (jj + 1)],
                                            woS[:, it * 1024 + 128 * j: it * 1024 + 128 * (j + 1)],
                                            oTb[:, qti * 1024 + it * 128: qti * 1024 + it * 128 + 128],
                                            start=(it == 0), stop=(it == ND - 1))
                    return mm

                def wo_ev(eng, buf, gi):
                    qti, half = divmod(gi, 2)
                    pX, pY = PAIRS[buf]
                    p = pX if half == 0 else pY
                    return eng.copy(attob[:, 512 * buf:512 * (buf + 1)], p[:, :])

                def wo_res(eng, buf, gi, c0=c0):
                    qti, half = divmod(gi, 2)
                    for jj in range(4):
                        j = 4 * half + jj
                        col = j * T + c0 + 128 * qti
                        mv = eng.tensor_tensor(xT[:, col:col + 128],
                                               attob[:, 512 * buf + 128 * jj: 512 * buf + 128 * (jj + 1)],
                                               xT[:, col:col + 128], op=ALU.add)
                    return mv

                pipelineN(block, S, plsems, list(range(NQC * 2)),
                          [dict(eng="t", fn=wo_fn), dict(eng="s", fn=wo_ev), dict(eng="v", fn=wo_res)])

            @block.gpsimd
            def _(g, l=l):
                wait_pl(g)
                for jt in range(NI):
                    r0 = 128 * jt
                    ri = min(128, DIM_INNER - r0)
                    S.inc(g.dma_start(out=regW[0:ri, jt * 1024: jt * 1024 + 1024],
                                      in_=w2_d[l, r0:r0 + ri, :]), "dma", 16)
                S.inc(g.dma_start(out=b1c[:, :], in_=b1r_d[l]), "dma", 16)
                S.inc(g.dma_start(out=b2c[:, :], in_=b2r_d[l]), "dma", 16)
                S.inc(g.dma_start(out=gcol[:, :], in_=lngr_d[l]), "dma", 16)
                S.inc(g.dma_start(out=bcol[:, :], in_=lnbr_d[l]), "dma", 16)
            ffn_dma = S.c["dma"]

            for mcc in range(NMC):
                mc0 = MC * mcc

                @block.vector
                def _(v, mcc=mcc, mc0=mc0, ffn_dma=ffn_dma):
                    wait_pl(v)
                    v.wait_ge(S.h["dma"], ffn_dma)
                    for j in range(ND):
                        if j < 4:
                            mv = v.tensor_copy(xsn[:, j * MC:(j + 1) * MC], xT[:, j * T + mc0: j * T + mc0 + MC])
                        else:
                            if mcc == 0:
                                v.memset(xsn[:, j * MC: j * MC + 1], 0.0)
                            else:
                                v.tensor_copy(xsn[:, j * MC: j * MC + 1], stash[:, j:j + 1])
                            mv = v.tensor_copy(xsn[:, j * MC + 1:(j + 1) * MC],
                                               xT[:, j * T + mc0: j * T + mc0 + MC - 1])
                    for j in range(4, 8):
                        mv = v.tensor_copy(stash[:, j:j + 1], xT[:, j * T + mc0 + MC - 1: j * T + mc0 + MC])
                    S.inc(mv, "v")
                xs_v = S.c["v"]

                def ln_stage(k, src, base_col=0, gb=True, wait_v=None, W=256):
                    def fn(eng, buf, gi, k=k, wait_v=wait_v):
                        g0 = base_col + W * gi
                        bW = 0
                        if k == 0:
                            if gi == 0:
                                eng.wait_ge(S.h["v"], xs_v if wait_v is None else wait_v)
                            for j in range(ND):
                                ms = eng.activation(sqg[:, bW + j * W: bW + (j + 1) * W],
                                                    src[:, j * MC + g0: j * MC + g0 + W], AF.Square)
                            return ms
                        if k == 1:
                            for j in range(ND):
                                eng.matmul(pStat[0:1, 0:W], onescol[:, :],
                                           src[:, j * MC + g0: j * MC + g0 + W],
                                           start=(j == 0), stop=(j == ND - 1))
                            for j in range(ND):
                                mm = eng.matmul(pStat[0:1, W:2 * W], onescol[:, :],
                                                sqg[:, bW + j * W: bW + (j + 1) * W],
                                                start=(j == 0), stop=(j == ND - 1))
                            return mm
                        if k == 2:
                            b7 = buf * 768
                            eng.copy(statrow[0:1, b7:b7 + W], pStat[0:1, 0:W])
                            return eng.activation(statrow[0:1, b7 + 256:b7 + 256 + W],
                                                  pStat[0:1, W:2 * W], AF.Copy, scale=1.0 / D)
                        if k == 3:
                            b7 = buf * 768
                            eng.tensor_scalar_mul(statrow[0:1, b7:b7 + W],
                                                  statrow[0:1, b7:b7 + W], 1.0 / D)
                            eng.tensor_tensor(statrow[0:1, b7 + 512:b7 + 512 + W],
                                              statrow[0:1, b7:b7 + W],
                                              statrow[0:1, b7:b7 + W], op=ALU.mult)
                            return eng.tensor_tensor(statrow[0:1, b7 + 256:b7 + 256 + W],
                                                     statrow[0:1, b7 + 256:b7 + 256 + W],
                                                     statrow[0:1, b7 + 512:b7 + 512 + W], op=ALU.subtract)
                        if k == 4:
                            b7 = buf * 768
                            eng.activation(statrow[0:1, b7 + 256:b7 + 256 + W],
                                           statrow[0:1, b7 + 256:b7 + 256 + W], AF.Ln, bias=epsc[:, :])
                            return eng.activation(statrow[0:1, b7 + 256:b7 + 256 + W],
                                                  statrow[0:1, b7 + 256:b7 + 256 + W], AF.Exp, scale=-0.5)
                        if k == 5:
                            b7 = buf * 768
                            bw5 = 0
                            eng.tensor_copy(muhi[0:1, bw5:bw5 + W], statrow[0:1, b7:b7 + W])
                            eng.tensor_copy(muhi[0:1, bw5 + 256:bw5 + 256 + W],
                                            statrow[0:1, b7 + 256:b7 + 256 + W])
                            eng.tensor_tensor(mulo[0:1, bw5:bw5 + W], statrow[0:1, b7:b7 + W],
                                              muhi[0:1, bw5:bw5 + W], op=ALU.subtract)
                            eng.tensor_tensor(mulo[0:1, bw5 + 256:bw5 + 256 + W],
                                              statrow[0:1, b7 + 256:b7 + 256 + W],
                                              muhi[0:1, bw5 + 256:bw5 + 256 + W], op=ALU.subtract)
                            return eng.tensor_copy(mulo16[0:1, bw5:bw5 + 2 * W], mulo[0:1, bw5:bw5 + 2 * W])
                        if k == 6:
                            bw5 = 0
                            eng.matmul(pBc[:, 0:2 * W], onesrow[0:1, :], muhi[0:1, bw5:bw5 + 2 * W],
                                       start=True, stop=False)
                            return eng.matmul(pBc[:, 0:2 * W], onesrow[0:1, :], mulo16[0:1, bw5:bw5 + 2 * W],
                                              start=False, stop=True)
                        if k == 7:
                            bw2 = buf * 256
                            eng.tensor_copy(mubcf[:, bw2:bw2 + W], pBc[:, 0:W])
                            mv = eng.tensor_copy(rstdf[:, bw2:bw2 + W], pBc[:, W:2 * W])
                            for j in range(ND):
                                eng.tensor_tensor(lnscr[:, 0:W], src[:, j * MC + g0: j * MC + g0 + W],
                                                  mubcf[:, bw2:bw2 + W], op=ALU.subtract)
                                if gb:
                                    eng.tensor_tensor(lnscr[:, 0:W], lnscr[:, 0:W],
                                                      rstdf[:, bw2:bw2 + W], op=ALU.mult)
                                    mv = eng.tensor_scalar(src[:, j * MC + g0: j * MC + g0 + W],
                                                           lnscr[:, 0:W], gcol[:, j:j + 1], bcol[:, j:j + 1],
                                                           op0=ALU.mult, op1=ALU.add)
                                else:
                                    mv = eng.tensor_tensor(src[:, j * MC + g0: j * MC + g0 + W],
                                                           lnscr[:, 0:W], rstdf[:, bw2:bw2 + W], op=ALU.mult)
                            return mv
                    return fn

                def mk_ln_stages(src, base_col=0, gb=True, wait_v=None):
                    return [dict(eng="s", fn=ln_stage(0, src, base_col, gb, wait_v), extra=[(1, 1)]),
                            dict(eng="t", fn=ln_stage(1, src, base_col, gb), extra=[(2, 1)]),
                            dict(eng="s", fn=ln_stage(2, src, base_col, gb)),
                            dict(eng="v", fn=ln_stage(3, src, base_col, gb)),
                            dict(eng="s", fn=ln_stage(4, src, base_col, gb)),
                            dict(eng="v", fn=ln_stage(5, src, base_col, gb), extra=[(6, 1)]),
                            dict(eng="t", fn=ln_stage(6, src, base_col, gb), extra=[(7, 1)]),
                            dict(eng="v", fn=ln_stage(7, src, base_col, gb))]

                pipelineN(block, S, plsems, list(range(NLG)), mk_ln_stages(xsn))

                for phase in (0, 1):
                    def w1dma_fn(eng, buf, gi, phase=phase, l=l):
                        r0 = 128 * gi
                        ri = min(128, DIM_INNER - r0)
                        wt = w1tp[:, 1024 * buf:1024 * (buf + 1)]
                        return eng.dma_start(
                            out=wt[:, 0:ND * ri].rearrange("p (j c) -> p j c", c=ri),
                            in_=w1_d[l, :, phase * DIM_INNER + r0: phase * DIM_INNER + r0 + ri]
                                .rearrange("(j p) c -> p j c", p=128))

                    def w1mm_fn(eng, buf, gi, phase=phase):
                        r0 = 128 * gi
                        ri = min(128, DIM_INNER - r0)
                        wt = w1tp[:, 1024 * buf:1024 * (buf + 1)]
                        p = pA if buf == 0 else pB
                        for j in range(ND):
                            mm = eng.matmul(p[0:ri, :], wt[:, j * ri: (j + 1) * ri],
                                            xsn[:, j * MC: j * MC + MC],
                                            start=(j == 0), stop=(j == ND - 1))
                        return mm

                    def a_ev(eng, buf, gi, phase=phase):
                        r0 = 128 * gi
                        ri = min(128, DIM_INNER - r0)
                        p = pA if buf == 0 else pB
                        bc = b1c[0:ri, (22 * phase + gi):(22 * phase + gi) + 1]
                        if phase == 0:
                            return eng.activation(AH[0:ri, gi * MC: gi * MC + MC], p[0:ri, :], AF.Identity, bias=bc)
                        return eng.activation(gsc[0:ri, buf * MC: buf * MC + MC], p[0:ri, :], AF.Gelu, bias=bc)

                    def g_mul(eng, buf, gi):
                        ri = min(128, DIM_INNER - 128 * gi)
                        return eng.tensor_tensor(AH[0:ri, gi * MC: gi * MC + MC],
                                                 AH[0:ri, gi * MC: gi * MC + MC],
                                                 gsc[0:ri, buf * MC: buf * MC + MC], op=ALU.mult)

                    stages = [dict(eng="g", fn=w1dma_fn, inc=16),
                              dict(eng="t", fn=w1mm_fn),
                              dict(eng="s", fn=a_ev)]
                    if phase == 1:
                        stages.append(dict(eng="v", fn=g_mul))
                    pipelineN(block, S, plsems, list(range(NI)), stages)

                def w2_fn(eng, buf, gi):
                    p = pA if buf == 0 else pB
                    for it in range(NI):
                        ri = min(128, DIM_INNER - 128 * it)
                        mm = eng.matmul(p[:, :], regW[0:ri, it * 1024 + 128 * gi: it * 1024 + 128 * (gi + 1)],
                                        AH[0:ri, it * MC: it * MC + MC],
                                        start=(it == 0), stop=(it == NI - 1))
                    return mm

                def w2_ev(eng, buf, gi):
                    p = pA if buf == 0 else pB
                    return eng.activation(ffnob[:, 512 * buf:512 * (buf + 1)], p[:, :],
                                          AF.Identity, bias=b2c[:, gi:gi + 1])

                def w2_res(eng, buf, gi, mc0=mc0):
                    col = gi * T + mc0
                    return eng.tensor_tensor(xT[:, col:col + MC], ffnob[:, 512 * buf:512 * buf + MC],
                                             xT[:, col:col + MC], op=ALU.add)

                pipelineN(block, S, plsems, list(range(ND)),
                          [dict(eng="t", fn=w2_fn), dict(eng="s", fn=w2_ev), dict(eng="v", fn=w2_res)])

        if do_final:
            fin_v_hist = []
            for grp in range(T // 256):
                g0 = 256 * grp

                @block.gpsimd
                def _(g, grp=grp, g0=g0):
                    if grp < 2:
                        wait_pl(g)
                    else:
                        g.wait_ge(S.h["v"], fin_v_hist[grp - 2])
                    for j in range(ND):
                        S.inc(g.dma_start(out=xinb[:, (grp % 2) * 2048 + j * 256:(grp % 2) * 2048 + (j + 1) * 256],
                                          in_=xin_d[128 * j:128 * (j + 1), g0:g0 + 256]), "dmb", 16)

                @block.vector
                def _(v, grp=grp, g0=g0):
                    S.wait(v, "dmb")
                    if grp < 2:
                        wait_pl(v)
                    for j in range(ND):
                        mv = v.tensor_tensor(xsn[:, j * MC + (grp % 2) * 256: j * MC + (grp % 2) * 256 + 256],
                                             xT[:, j * T + g0: j * T + g0 + 256],
                                             xinb[:, (grp % 2) * 2048 + j * 256:(grp % 2) * 2048 + (j + 1) * 256],
                                             op=ALU.add)
                    S.inc(mv, "v")
                sres_v = S.c["v"]
                fin_v_hist.append(sres_v)

                def ln_stage2(*a, **k):
                    raise RuntimeError
                pipelineN(block, S, plsems, [0],
                          mk_ln_stages(xsn, base_col=(grp % 2) * 256, gb=False, wait_v=sres_v))

                @block.gpsimd
                def _(g, grp=grp, g0=g0):
                    S.wait(g, "pl7")
                    for j in range(ND):
                        S.inc(g.dma_start(out=out_d[128 * j:128 * (j + 1), g0:g0 + 256],
                                          in_=xsn[:, j * MC + (grp % 2) * 256: j * MC + (grp % 2) * 256 + 256]), "st", 16)
        else:
            @block.gpsimd
            def _(g):
                wait_pl(g)
                for j in range(ND):
                    S.inc(g.dma_start(out=out_d[128 * j:128 * (j + 1), :], in_=xT[:, j * T:(j + 1) * T]), "st", 16)

        @block.gpsimd
        def _(g):
            S.wait(g, "st")

    return nc


def phase_b_consts():
    half = np.arange(0, DH, 2, dtype=np.float64)
    inv_freq = 1.0 / 10000.0 ** (half / DH)
    t = np.arange(T, dtype=np.float64)
    freqs = t[:, None] * inv_freq[None, :]
    freqs = np.concatenate([freqs, freqs], -1)
    base_scale = (half + 0.4 * DH) / (1.4 * DH)
    power = (t - T // 2) / (256 // 2)
    scale = base_scale[None, :] ** power[:, None]
    scale = np.concatenate([scale, scale], -1)
    cos = np.cos(freqs); sin = np.sin(freqs)
    sgn = np.where(np.arange(DH) < DH // 2, -1.0, 1.0)
    csq = (cos * scale).T; snq = (sin * scale).T * sgn[:, None]
    csk = (cos / scale).T; snk = (sin / scale).T * sgn[:, None]
    def rep(a):
        return np.concatenate([a, a], 0).astype(np.float32)
    blk16 = np.zeros((128, 8 * 16), np.float32)
    for j in range(ND):
        blk16[0:64, 16 * j + 2 * j] = 1.0
        blk16[64:128, 16 * j + 2 * j + 1] = 1.0
    sel16 = np.zeros((16, 8 * 128), np.float32)
    for j in range(ND):
        sel16[2 * j, 128 * j: 128 * j + 64] = 1.0
        sel16[2 * j + 1, 128 * j + 64: 128 * j + 128] = 1.0
    permm = np.zeros((128, 128), np.float32)
    for base in (0, 64):
        for d in range(DH):
            sd = d + 32 if d < 32 else d - 32
            permm[base + sd, base + d] = 1.0
    return dict(csq=rep(csq), snq=rep(snq), csk=rep(csk), snk=rep(snk),
                blk16=blk16, sel16=sel16, perm=permm)



def build_phase_c():
    nc = bass.Bass("TRN2", target_bir_lowering=False)
    x_d = nc.dram_tensor("xcin", [D, T], BF16, kind="ExternalInput").ap()
    wkT_d = nc.dram_tensor("wkT", [D, D], F32, kind="ExternalInput").ap()
    wvT_d = nc.dram_tensor("wvT", [D, D], F32, kind="ExternalInput").ap()
    wrT_d = nc.dram_tensor("wrT", [D, D], F32, kind="ExternalInput").ap()
    woT_d = nc.dram_tensor("woT", [D, D], F32, kind="ExternalInput").ap()
    muk_d = nc.dram_tensor("mukc", [128, 8], F32, kind="ExternalInput").ap()
    muv_d = nc.dram_tensor("muvc", [128, 8], F32, kind="ExternalInput").ap()
    mur_d = nc.dram_tensor("murc", [128, 8], F32, kind="ExternalInput").ap()
    wc_d = nc.dram_tensor("wcc", [128, 8], F32, kind="ExternalInput").ap()
    uc_d = nc.dram_tensor("ucc", [128, 8], F32, kind="ExternalInput").ap()
    out_d = nc.dram_tensor("xcT", [D, T], BF16, kind="ExternalOutput").ap()
    SB = 512

    with ExitStack() as ctx:
        E = ctx.enter_context
        xT = E(nc.sbuf_tensor("xT", [128, ND * T], BF16))
        wk = E(nc.sbuf_tensor("wkS", [128, ND * D], BF16))
        wv = E(nc.sbuf_tensor("wvS", [128, ND * D], BF16))
        wr = E(nc.sbuf_tensor("wrS", [128, ND * D], BF16))
        wo = E(nc.sbuf_tensor("woSc", [128, ND * D], BF16))
        mucols = E(nc.sbuf_tensor("mucols", [128, 48], F32))
        ewc = E(nc.sbuf_tensor("ewc", [128, 8], F32))
        euc = E(nc.sbuf_tensor("euc", [128, 8], F32))
        mixb = E(nc.sbuf_tensor("mixb", [128, ND * SB], BF16))
        ekb = E(nc.sbuf_tensor("ekb", [128, ND * SB], BF16))
        ekvb = E(nc.sbuf_tensor("ekvb", [128, ND * SB], BF16))
        srb = E(nc.sbuf_tensor("srb", [128, ND * SB], BF16))
        outb = E(nc.sbuf_tensor("outb", [128, ND * SB], BF16))
        Pden = E(nc.sbuf_tensor("Pden", [128, ND * SB], F32))
        Pnum = E(nc.sbuf_tensor("Pnum", [128, ND * SB], F32))
        dscr = E(nc.sbuf_tensor("dscr", [128, SB], F32))
        nscr = E(nc.sbuf_tensor("nscr", [128, SB], F32))
        onescol = E(nc.sbuf_tensor("onescol", [128, 1], BF16))
        onesrow = E(nc.sbuf_tensor("onesrow", [1, 128], BF16))
        sqg = mixb[:, 0:2048]
        statrow = E(nc.sbuf_tensor("statrow", [1, 768], F32))
        muhi = srb[0:1, 2048:2560]
        mulo = dscr[0:1, 0:512]
        mulo16 = srb[0:1, 2560:3072]
        mubcf = nscr[:, 0:256]
        rstdf = nscr[:, 256:512]
        lnscr = mixb[:, 2048:2560].bitcast(F32)
        sres = ekb[:, 0:2048]
        epsc = E(nc.sbuf_tensor("epsc", [1, 1], F32))
        pA = E(nc.psum_tensor("pA", [128, 512], F32))
        pB = E(nc.psum_tensor("pB", [128, 512], F32))
        pStat = E(nc.psum_tensor("pStatc", [1, 512], F32))
        pBc = E(nc.psum_tensor("pBcc", [128, 512], F32))
        sems = {}
        for n in ("dma", "dmb", "t", "v", "s", "g", "st"):
            sems[n] = E(nc.semaphore(n))
        block = E(nc.Block())
        S = Sems()
        for n, h in sems.items():
            S.add(n, h)

        @block.gpsimd
        def _(g):
            for j in range(ND):
                S.inc(g.dma_start(out=xT[:, j * T:(j + 1) * T], in_=x_d[128 * j:128 * (j + 1), :]), "dma", 16)
            for wS, wD in ((wk, wkT_d), (wv, wvT_d), (wr, wrT_d), (wo, woT_d)):
                for j in range(ND):
                    S.inc(g.dma_start(out=wS[:, j * D:(j + 1) * D], in_=wD[128 * j:128 * (j + 1), :]), "dma", 16)
            S.inc(g.dma_start(out=mucols[:, 0:8], in_=muk_d), "dma", 16)
            S.inc(g.dma_start(out=mucols[:, 16:24], in_=muv_d), "dma", 16)
            S.inc(g.dma_start(out=mucols[:, 32:40], in_=mur_d), "dma", 16)
            S.inc(g.dma_start(out=ewc[:, :], in_=wc_d), "dma", 16)
            S.inc(g.dma_start(out=euc[:, :], in_=uc_d), "dma", 16)
            g.memset(onescol[:, :], 1.0)
            g.memset(onesrow[:, :], 1.0)
            S.inc(g.memset(epsc[:, :], LNEPS), "g")
        ld = S.c["dma"]
        cg = S.c["g"]

        @block.vector
        def _(v):
            v.wait_ge(S.h["dma"], ld)
            v.wait_ge(S.h["g"], cg)
            for o in (0, 16, 32):
                v.tensor_scalar(mucols[:, o + 8:o + 16], mucols[:, o:o + 8], -1.0, 1.0,
                                op0=ALU.mult, op1=ALU.add)
            v.memset(Pden[:, :], 0.0)
            S.inc(v.memset(Pnum[:, :], 0.0), "v")

        @block.scalar
        def _(s):
            S.wait(s, "v")
            s.activation(ewc[:, :], ewc[:, :], AF.Exp)
            s.activation(ewc[:, :], ewc[:, :], AF.Exp, scale=-1.0)
            S.inc(s.activation(euc[:, :], euc[:, :], AF.Exp), "s")

        @block.tensor
        def _(t):
            S.wait(t, "s")

        for b in range(8):
            bc0 = b * SB

            for ti, (wS, muoff) in enumerate(((wk, 0), (wv, 16), (wr, 32))):
                @block.vector
                def _(v, b=b, bc0=bc0, muoff=muoff, ti=ti):
                    if b == 0 and ti == 0:
                        S.wait(v, "s")
                    S.wait(v, "t")
                    for j in range(ND):
                        xb = xT[:, j * T + bc0: j * T + bc0 + SB]
                        if b == 0:
                            mv = v.tensor_scalar_mul(mixb[:, j * SB:(j + 1) * SB], xb,
                                                     mucols[:, muoff + j:muoff + j + 1])
                        else:
                            xp = xT[:, j * T + bc0 - SB: j * T + bc0]
                            v.tensor_scalar_mul(mixb[:, j * SB:(j + 1) * SB], xb,
                                                mucols[:, muoff + j:muoff + j + 1])
                            mv = v.scalar_tensor_tensor(mixb[:, j * SB:(j + 1) * SB], xp,
                                                        mucols[:, muoff + 8 + j:muoff + 8 + j + 1],
                                                        mixb[:, j * SB:(j + 1) * SB],
                                                        op0=ALU.mult, op1=ALU.add)
                    S.inc(mv, "v")

                s_base = S.c["s"]

                @block.tensor
                def _(t, b=b, ti=ti, wS=wS, s_base=s_base):
                    S.wait(t, "v")
                    S.wait(t, "s", s_base)
                    for m in range(ND):
                        p = pA if m % 2 == 0 else pB
                        if m >= 2:
                            t.wait_ge(S.h["s"], s_base + m - 1)
                        for j in range(ND):
                            mm = t.matmul(p[:, :], wS[:, j * D + 128 * m: j * D + 128 * (m + 1)],
                                          mixb[:, j * SB:(j + 1) * SB],
                                          start=(j == 0), stop=(j == ND - 1))
                        S.inc(mm, "t")

                for m in range(ND):
                    @block.scalar
                    def _(s, b=b, ti=ti, m=m):
                        S.wait(s, "t", S.c["t"] - (ND - 1 - m))
                        if ti == 2 and m == 0:
                            S.wait(s, "v")
                        p = pA if m % 2 == 0 else pB
                        if ti == 0:
                            ms = s.activation(ekb[:, m * SB:(m + 1) * SB], p[:, :], AF.Exp)
                        elif ti == 1:
                            ms = s.copy(srb[:, m * SB:(m + 1) * SB], p[:, :])
                        else:
                            ms = s.activation(srb[:, m * SB:(m + 1) * SB], p[:, :], AF.Sigmoid)
                        S.inc(ms, "s")

                if ti == 1:
                    @block.vector
                    def _(v, b=b):
                        S.wait(v, "s")
                        for m in range(ND):
                            mv = v.tensor_tensor(ekvb[:, m * SB:(m + 1) * SB],
                                                 ekb[:, m * SB:(m + 1) * SB],
                                                 srb[:, m * SB:(m + 1) * SB], op=ALU.mult)
                        S.inc(mv, "v")

            for m in range(ND):
                @block.vector
                def _(v, b=b, m=m):
                    if m == 0:
                        S.wait(v, "s")
                    else:
                        S.wait(v, "s")
                    sl = slice(m * SB, (m + 1) * SB)
                    v.scalar_tensor_tensor(dscr[:, :], ekb[:, sl], euc[:, m:m + 1], Pden[:, sl],
                                           op0=ALU.mult, op1=ALU.add)
                    S.inc(v.scalar_tensor_tensor(nscr[:, :], ekvb[:, sl], euc[:, m:m + 1], Pnum[:, sl],
                                                 op0=ALU.mult, op1=ALU.add), "v")

                @block.scalar
                def _(s, b=b, m=m):
                    S.wait(s, "v")
                    s.activation(dscr[:, :], dscr[:, :], AF.Ln)
                    S.inc(s.activation(dscr[:, :], dscr[:, :], AF.Exp, scale=-1.0), "s")

                @block.vector
                def _(v, b=b, m=m):
                    S.wait(v, "s")
                    sl = slice(m * SB, (m + 1) * SB)
                    v.tensor_tensor(nscr[:, :], nscr[:, :], dscr[:, :], op=ALU.mult)
                    v.tensor_tensor(srb[:, sl], srb[:, sl], nscr[:, :], op=ALU.mult)
                    v.scalar_tensor_tensor(Pden[:, sl], Pden[:, sl], ewc[:, m:m + 1], ekb[:, sl],
                                           op0=ALU.mult, op1=ALU.add)
                    mv = v.scalar_tensor_tensor(Pnum[:, sl], Pnum[:, sl], ewc[:, m:m + 1], ekvb[:, sl],
                                                op0=ALU.mult, op1=ALU.add)
                    S.inc(mv, "v")

            so_base = S.c["s"]

            @block.tensor
            def _(t, b=b, so_base=so_base):
                S.wait(t, "v")
                for m in range(ND):
                    p = pA if m % 2 == 0 else pB
                    if m >= 2:
                        t.wait_ge(S.h["s"], so_base + m - 1)
                    for j in range(ND):
                        mm = t.matmul(p[:, :], wo[:, j * D + 128 * m: j * D + 128 * (m + 1)],
                                      srb[:, j * SB:(j + 1) * SB],
                                      start=(j == 0), stop=(j == ND - 1))
                    S.inc(mm, "t")

            for m in range(ND):
                @block.scalar
                def _(s, b=b, m=m):
                    S.wait(s, "t", S.c["t"] - (ND - 1 - m))
                    p = pA if m % 2 == 0 else pB
                    S.inc(s.copy(outb[:, m * SB:(m + 1) * SB], p[:, :]), "s")

            for gg in range(SB // 256):
                @block.vector
                def _(v, b=b, bc0=bc0, gg=gg):
                    S.wait(v, "s")
                    S.wait(v, "st")
                    for j in range(ND):
                        mv = v.tensor_tensor(sres[:, j * 256:(j + 1) * 256],
                                             outb[:, j * SB + gg * 256: j * SB + gg * 256 + 256],
                                             xT[:, j * T + bc0 + gg * 256: j * T + bc0 + gg * 256 + 256],
                                             op=ALU.add)
                    S.inc(mv, "v")

                emit_ln_c(nc, block, S, sres, 256,
                          dict(sqg=sqg, onescol=onescol, onesrow=onesrow, statrow=statrow,
                               muhi=muhi, mulo=mulo, mulo16=mulo16, mubcf=mubcf, rstdf=rstdf,
                               lnscr=lnscr, epsc=epsc, pStat=pStat, pBc=pBc))

                @block.gpsimd
                def _(g, b=b, bc0=bc0, gg=gg):
                    S.wait(g, "v")
                    for j in range(ND):
                        S.inc(g.dma_start(out=out_d[128 * j:128 * (j + 1), bc0 + gg * 256: bc0 + gg * 256 + 256],
                                          in_=sres[:, j * 256:(j + 1) * 256]), "st", 16)

        @block.gpsimd
        def _(g):
            S.wait(g, "st")

    return nc


def emit_ln_c(nc, block, S, sres, W, env):
    sqg = env["sqg"]; onescol = env["onescol"]; onesrow = env["onesrow"]
    statrow = env["statrow"]; muhi = env["muhi"]; mulo = env["mulo"]; mulo16 = env["mulo16"]
    mubcf = env["mubcf"]; rstdf = env["rstdf"]; lnscr = env["lnscr"]; epsc = env["epsc"]
    pStat = env["pStat"]; pBc = env["pBc"]

    @block.scalar
    def _(s):
        S.wait(s, "v")
        for j in range(ND):
            ms = s.activation(sqg[:, j * W:(j + 1) * W], sres[:, j * W:(j + 1) * W], AF.Square)
        S.inc(ms, "s")

    @block.tensor
    def _(t):
        S.wait(t, "s")
        for j in range(ND):
            t.matmul(pStat[0:1, 0:W], onescol[:, :], sres[:, j * W:(j + 1) * W],
                     start=(j == 0), stop=(j == ND - 1))
        for j in range(ND):
            mm = t.matmul(pStat[0:1, W:2 * W], onescol[:, :], sqg[:, j * W:(j + 1) * W],
                          start=(j == 0), stop=(j == ND - 1))
        S.inc(mm, "t")

    @block.scalar
    def _(s):
        S.wait(s, "t")
        s.copy(statrow[0:1, 0:W], pStat[0:1, 0:W])
        S.inc(s.activation(statrow[0:1, 256:256 + W], pStat[0:1, W:2 * W], AF.Copy, scale=1.0 / D), "s")

    @block.vector
    def _(v):
        S.wait(v, "s")
        v.tensor_scalar_mul(statrow[0:1, 0:W], statrow[0:1, 0:W], 1.0 / D)
        v.tensor_tensor(statrow[0:1, 512:512 + W], statrow[0:1, 0:W], statrow[0:1, 0:W], op=ALU.mult)
        S.inc(v.tensor_tensor(statrow[0:1, 256:256 + W], statrow[0:1, 256:256 + W],
                              statrow[0:1, 512:512 + W], op=ALU.subtract), "v")

    @block.scalar
    def _(s):
        S.wait(s, "v")
        s.activation(statrow[0:1, 256:256 + W], statrow[0:1, 256:256 + W], AF.Ln, bias=epsc[:, :])
        S.inc(s.activation(statrow[0:1, 256:256 + W], statrow[0:1, 256:256 + W], AF.Exp, scale=-0.5), "s")

    @block.vector
    def _(v):
        S.wait(v, "s")
        v.tensor_copy(muhi[0:1, 0:W], statrow[0:1, 0:W])
        v.tensor_copy(muhi[0:1, 256:256 + W], statrow[0:1, 256:256 + W])
        v.tensor_tensor(mulo[0:1, 0:W], statrow[0:1, 0:W], muhi[0:1, 0:W], op=ALU.subtract)
        v.tensor_tensor(mulo[0:1, 256:256 + W], statrow[0:1, 256:256 + W], muhi[0:1, 256:256 + W],
                        op=ALU.subtract)
        S.inc(v.tensor_copy(mulo16[0:1, 0:2 * W], mulo[0:1, 0:2 * W]), "v")

    @block.tensor
    def _(t):
        S.wait(t, "v")
        t.matmul(pBc[:, 0:2 * W], onesrow[0:1, :], muhi[0:1, 0:2 * W], start=True, stop=False)
        S.inc(t.matmul(pBc[:, 0:2 * W], onesrow[0:1, :], mulo16[0:1, 0:2 * W], start=False, stop=True), "t")

    @block.vector
    def _(v):
        S.wait(v, "t")
        v.tensor_copy(mubcf[:, 0:W], pBc[:, 0:W])
        v.tensor_copy(rstdf[:, 0:W], pBc[:, W:2 * W])
        for j in range(ND):
            v.tensor_tensor(lnscr[:, 0:W], sres[:, j * W:(j + 1) * W], mubcf[:, 0:W], op=ALU.subtract)
            mv = v.tensor_tensor(sres[:, j * W:(j + 1) * W], lnscr[:, 0:W], rstdf[:, 0:W], op=ALU.mult)
        S.inc(mv, "v")



NKV = 4
NEXP = 8
CAP = 10240.0
RMSEPS = 1e-6


def build_phase_d(stage=4):
    nc = bass.Bass("TRN2", target_bir_lowering=False)
    x_d = nc.dram_tensor("xdin", [D, T], BF16, kind="ExternalInput").ap()
    qw_d = nc.dram_tensor("qw", [D, D], F32, kind="ExternalInput").ap()
    kw_d = nc.dram_tensor("kw", [D, 256], F32, kind="ExternalInput").ap()
    vw_d = nc.dram_tensor("vw", [D, 256], F32, kind="ExternalInput").ap()
    ow_d = nc.dram_tensor("ow", [D, D], F32, kind="ExternalInput").ap()
    qbc_d = nc.dram_tensor("qbc", [128, 8], F32, kind="ExternalInput").ap()
    kbc_d = nc.dram_tensor("kbc", [128, 2], F32, kind="ExternalInput").ap()
    vbr_d = nc.dram_tensor("vbr", [1, 256], F32, kind="ExternalInput").ap()
    obc_d = nc.dram_tensor("obc", [128, 8], F32, kind="ExternalInput").ap()
    gw_d = nc.dram_tensor("gatew", [D, 8], F32, kind="ExternalInput").ap()
    w1_d = nc.dram_tensor("w1e", [NEXP, D, 2048], F32, kind="ExternalInput").ap()
    w2_d = nc.dram_tensor("w2e", [NEXP, 2048, D], F32, kind="ExternalInput").ap()
    n1c_d = nc.dram_tensor("n1c", [128, 8], F32, kind="ExternalInput").ap()
    n2c_d = nc.dram_tensor("n2c", [128, 8], F32, kind="ExternalInput").ap()
    cmask_d = nc.dram_tensor("coremask", [8, 8], F32, kind="ExternalInput").ap()
    out_d = nc.dram_tensor("xdT", [D, T], F32, kind="ExternalOutput").ap()
    cnt_in = nc.dram_tensor("cnt_in", [1, 8], F32)
    cnt_out = nc.dram_tensor("cnt_out", [8, 8], F32)

    with ExitStack() as ctx:
        E = ctx.enter_context
        xT = E(nc.sbuf_tensor("xT", [128, ND * T], BF16))
        poolD = E(nc.sbuf_tensor("poolD", [128, 53504], BF16))
        qwS = poolD[:, 0:8192]
        owS = poolD[:, 8192:16384]
        qTq = poolD[:, 16384:20480]
        sqq = poolD[:, 20480:24576]
        oTq = poolD[:, 24576:28672]
        vS = poolD[:, 28672:37024]
        K65 = [poolD[0:128, 37024 + 4096 * i: 37024 + 4096 * (i + 1)] for i in range(4)]
        accM = poolD[:, 0:8192]
        HM = poolD[:, 8192:16384]
        w1t0 = poolD[:, 16384:18432]
        w1t1 = poolD[:, 18432:20480]
        w2S = poolD[:, 20480:36864]
        geT = poolD[0:8, 36864:40960]
        xinb2 = poolD[:, 40960:45056]
        lnrows = poolD[:, 45056:53504]
        kwS = E(nc.sbuf_tensor("kwS", [128, ND * 256], BF16))
        vwS = E(nc.sbuf_tensor("vwS", [128, ND * 256], BF16))
        gwS = E(nc.sbuf_tensor("gwS", [128, ND * 8], BF16))
        q65 = E(nc.sbuf_tensor("q65", [128, 512], BF16))
        et0 = E(nc.sbuf_tensor("et0", [128, 512], BF16))
        et1 = E(nc.sbuf_tensor("et1", [128, 512], BF16))
        rdrow = E(nc.sbuf_tensor("rdrow", [1, 512], F32))
        rdrow16 = E(nc.sbuf_tensor("rdrow16", [1, 512], BF16))
        rdbc = E(nc.sbuf_tensor("rdbc", [128, 512], F32))
        kssf = E(nc.sbuf_tensor("kssf", [128, 512], BF16))
        kmx = E(nc.sbuf_tensor("kmx", [1, 8], F32))
        bcrow = E(nc.sbuf_tensor("bcrow", [1, 8], F32))
        brow = E(nc.sbuf_tensor("brow", [1, 512], F32))
        ones64 = E(nc.sbuf_tensor("ones64", [128, 2], BF16))
        qbcol = E(nc.sbuf_tensor("qbcol", [128, 8], F32))
        kbcol = E(nc.sbuf_tensor("kbcol", [128, 2], F32))
        vbrow = E(nc.sbuf_tensor("vbrow", [1, 256], BF16))
        obcol = E(nc.sbuf_tensor("obcol", [128, 8], F32))
        n1col = E(nc.sbuf_tensor("n1col", [128, 8], F32))
        n2col = E(nc.sbuf_tensor("n2col", [128, 8], F32))
        iotaP = E(nc.sbuf_tensor("iotaP", [128, 8], F32))
        iotaN = E(nc.sbuf_tensor("iotaN", [128, 8], F32))
        iotaM = E(nc.sbuf_tensor("iotaM", [128, 8], F32))
        rows8 = E(nc.sbuf_tensor("rows8s", [8, 8 * 128], BF16))
        cmaskS = E(nc.sbuf_tensor("cmaskS", [8, 8], F32))
        bigneg = E(nc.sbuf_tensor("bigneg", [128, 1], F32))
        capm1col = E(nc.sbuf_tensor("capm1col", [128, 1], F32))
        epswc = E(nc.sbuf_tensor("epswc", [128, 1], F32))
        lgB = E(nc.sbuf_tensor("lgB", [128, 280], F32))
        mvalD = E(nc.sbuf_tensor("mvalD", [128, 64], F32))
        pb64 = E(nc.sbuf_tensor("pb64", [128, 64], F32))
        mxG = E(nc.sbuf_tensor("mxG", [128, 32], F32))
        m1G = E(nc.sbuf_tensor("m1G", [128, 16], F32))
        candG = E(nc.sbuf_tensor("candG", [128, 32], F32))
        hhG = E(nc.sbuf_tensor("hhG", [128, 32], F32))
        mskG = E(nc.sbuf_tensor("mskG", [128, 32], F32))
        zero8 = E(nc.sbuf_tensor("zero8", [128, 8], F32))
        zrow = E(nc.sbuf_tensor("zrow", [1, 8], F32))
        a0b = E(nc.sbuf_tensor("a0b", [128, 32 * 8], F32))
        a1b = E(nc.sbuf_tensor("a1b", [128, 32 * 8], F32))
        cnt16 = E(nc.sbuf_tensor("cnt16", [128, 32 * 8], BF16))
        ge16 = E(nc.sbuf_tensor("ge16", [128, 32 * 8], BF16))
        gw01 = E(nc.sbuf_tensor("gw01", [128, 64], F32))
        cnt32 = E(nc.sbuf_tensor("cnt32", [32, 32], F32))
        cnthi = E(nc.sbuf_tensor("cnthi", [8, 8], BF16))
        cntlo = E(nc.sbuf_tensor("cntlo", [8, 8], BF16))
        cm16 = E(nc.sbuf_tensor("cm16", [8, 1], BF16))
        runrow = E(nc.sbuf_tensor("runrow", [1, 8], F32))
        runhi = E(nc.sbuf_tensor("runhi", [1, 8], BF16))
        runlo16 = E(nc.sbuf_tensor("runlo16", [1, 8], BF16))
        exlf = E(nc.sbuf_tensor("exlf", [128, 8], F32))
        rnk = E(nc.sbuf_tensor("rnk", [128, 8], F32))
        gbcf = E(nc.sbuf_tensor("gbcf", [128, 512], F32))
        yscr = E(nc.sbuf_tensor("yscr", [128, 512], F32))
        triS = E(nc.sbuf_tensor("triS", [128, 128], BF16))
        onescol = E(nc.sbuf_tensor("onescol", [128, 1], BF16))
        onesrow = E(nc.sbuf_tensor("onesrow", [1, 128], BF16))
        ident = E(nc.sbuf_tensor("identd", [128, 128], BF16))
        sqg = lnrows[:, 0:2048]
        statrow = lnrows[0:1, 2048:3584].bitcast(F32)
        muhi = lnrows[0:1, 3584:4096]
        mulo = lnrows[0:1, 4096:5120].bitcast(F32)
        mulo16 = lnrows[0:1, 5120:5632]
        mubcf = lnrows[:, 5632:6144].bitcast(F32)
        rstdf = lnrows[:, 6144:6656].bitcast(F32)
        lnscr = E(nc.sbuf_tensor("lnscrd", [128, 256], F32))
        sres = E(nc.sbuf_tensor("sresd", [128, 8 * 256], BF16))
        epsc = E(nc.sbuf_tensor("epscd", [1, 1], F32))
        epsr = E(nc.sbuf_tensor("epsrd", [1, 1], F32))
        eps30 = E(nc.sbuf_tensor("eps30", [1, 1], F32))
        epsw = E(nc.sbuf_tensor("epsw", [1, 1], F32))
        obscr = E(nc.sbuf_tensor("obscr", [128, 512], BF16))
        pA = E(nc.psum_tensor("pA", [128, 512], F32))
        pB = E(nc.psum_tensor("pB", [128, 512], F32))
        pC = E(nc.psum_tensor("pC", [128, 512], F32))
        pO = E(nc.psum_tensor("pO", [128, 512], F32))
        pR8 = E(nc.psum_tensor("pR8", [128, 512], F32))
        pT0 = E(nc.psum_tensor("pT0d", [128, 1024], BF16))
        pStat = E(nc.psum_tensor("pStatd", [1, 512], F32))
        pBc = E(nc.psum_tensor("pBcd", [128, 512], F32))
        sems = {}
        for n in ("dma", "dmb", "t", "v", "s", "g", "st", "cc"):
            sems[n] = E(nc.semaphore(n))
        block = E(nc.Block())
        S = Sems()
        for n, h in sems.items():
            S.add(n, h)
        lnenv = dict(sqg=sqg, onescol=onescol, onesrow=onesrow, statrow=statrow,
                     muhi=muhi, mulo=mulo, mulo16=mulo16, mubcf=mubcf, rstdf=rstdf,
                     lnscr=lnscr, epsc=epsc, pStat=pStat, pBc=pBc)

        @block.gpsimd
        def _(g):
            for j in range(ND):
                S.inc(g.dma_start(out=xT[:, j * T:(j + 1) * T], in_=x_d[128 * j:128 * (j + 1), :]), "dma", 16)
            for j in range(ND):
                S.inc(g.dma_start(out=qwS[:, j * 1024:(j + 1) * 1024], in_=qw_d[128 * j:128 * (j + 1), :]), "dma", 16)
                S.inc(g.dma_start(out=owS[:, j * 1024:(j + 1) * 1024], in_=ow_d[128 * j:128 * (j + 1), :]), "dma", 16)
                S.inc(g.dma_start(out=kwS[:, j * 256:(j + 1) * 256], in_=kw_d[128 * j:128 * (j + 1), :]), "dma", 16)
                S.inc(g.dma_start(out=vwS[:, j * 256:(j + 1) * 256], in_=vw_d[128 * j:128 * (j + 1), :]), "dma", 16)
                S.inc(g.dma_start(out=gwS[:, j * 8:(j + 1) * 8], in_=gw_d[128 * j:128 * (j + 1), :]), "dma", 16)
            for dst, srcd in ((qbcol, qbc_d), (kbcol, kbc_d), (obcol, obc_d), (n1col, n1c_d),
                              (n2col, n2c_d), (cmaskS, cmask_d)):
                S.inc(g.dma_start(out=dst[:, :], in_=srcd), "dma", 16)
            S.inc(g.dma_start(out=vbrow[:, :], in_=vbr_d), "dma", 16)
            g.iota(iotaP[:, :], pattern=[[1, 8]], base=0, channel_multiplier=0,
                   allow_small_or_imprecise_dtypes=True)
            g.iota(iotaN[:, :], pattern=[[-1, 8]], base=0, channel_multiplier=0,
                   allow_small_or_imprecise_dtypes=True)
            g.iota(iotaM[:, :], pattern=[[1, 8]], base=-1000, channel_multiplier=0,
                   allow_small_or_imprecise_dtypes=True)
            g.memset(rows8[:, :], 1.0)
            g.affine_select(out=rows8[:, :], in_=rows8[:, :], compare_op=ALU.is_ge,
                            fill=0.0, base=0, pattern=[[1, 1024]], channel_multiplier=-128)
            g.affine_select(out=rows8[:, :], in_=rows8[:, :], compare_op=ALU.is_ge,
                            fill=0.0, base=127, pattern=[[-1, 1024]], channel_multiplier=128)
            make_identity(nc, ident[:, :])
            g.memset(triS[:, :], 1.0)
            g.affine_select(out=triS[:, :], in_=triS[:, :], compare_op=ALU.is_ge,
                            fill=0.0, base=-1, pattern=[[1, 128]], channel_multiplier=-1)
            g.memset(onescol[:, :], 1.0)
            g.memset(onesrow[:, :], 1.0)
            g.memset(ones64[:, :], 0.0)
            g.memset(ones64[0:64, 0:1], 1.0)
            g.memset(ones64[64:128, 1:2], 1.0)
            g.memset(bigneg[:, :], -1e30)
            g.memset(capm1col[:, :], CAP - 1.0)
            g.memset(epswc[:, :], 1e-6)
            g.memset(zero8[:, :], 0.0)
            g.memset(zrow[:, :], 0.0)
            g.memset(epsc[:, :], LNEPS)
            g.memset(epsr[:, :], RMSEPS)
            g.memset(epsw[:, :], 1e-6)
            g.memset(kmx[:, :], 0.0)
            S.inc(g.memset(eps30[:, :], 1e-30), "g")
        ld = S.c["dma"]
        cg = S.c["g"]
        for engn in ("tensor", "vector", "scalar"):
            @getattr(block, engn)
            def _(e):
                e.wait_ge(S.h["dma"], ld)
                e.wait_ge(S.h["g"], cg)

        @block.vector
        def _(v):
            for tt in range(32):
                for h in range(NKV):
                    mv = v.memset(vS[:, tt * 260 + 65 * h + 64: tt * 260 + 65 * h + 65], 1.0)
            for h in range(NKV):
                mv = v.memset(K65[h][64:65, :], 1.0)
            S.inc(mv, "v")

        for mt in range(2):
            for ch in range(8):
                @block.tensor
                def _(t, mt=mt, ch=ch):
                    S.wait(t, "v")
                    for j in range(ND):
                        mm = t.matmul(pA[:, :], kwS[:, j * 256 + 128 * mt: j * 256 + 128 * (mt + 1)],
                                      xT[:, j * T + 512 * ch: j * T + 512 * (ch + 1)],
                                      start=(j == 0), stop=(j == ND - 1))
                    S.inc(mm, "t")

                @block.scalar
                def _(s, mt=mt, ch=ch):
                    S.wait(s, "t")
                    s.activation(K65[2 * mt][0:64, 512 * ch:512 * (ch + 1)], pA[0:64, :],
                                 AF.Identity, bias=kbcol[0:64, mt:mt + 1])
                    S.inc(s.activation(K65[2 * mt + 1][0:64, 512 * ch:512 * (ch + 1)], pA[64:128, :],
                                       AF.Identity, bias=kbcol[64:128, mt:mt + 1]), "s")

                for hh in range(2):
                    h = 2 * mt + hh

                    @block.vector
                    def _(v, h=h, ch=ch):
                        S.wait(v, "s")
                        S.inc(v.tensor_tensor(kssf[0:64, :], K65[h][0:64, 512 * ch:512 * (ch + 1)],
                                              K65[h][0:64, 512 * ch:512 * (ch + 1)], op=ALU.mult), "v")

                    @block.tensor
                    def _(t, h=h, ch=ch):
                        S.wait(t, "v")
                        S.inc(t.matmul(pR8[0:1, 0:512], onescol[0:64, :], kssf[0:64, :],
                                       start=True, stop=True), "t")

                    @block.vector
                    def _(v, h=h, ch=ch):
                        S.wait(v, "t")
                        v.tensor_reduce(brow[0:1, 0:1], pR8[0:1, 0:512], axis=AX.X, op=ALU.max)
                        S.inc(v.tensor_tensor(kmx[0:1, h:h + 1], kmx[0:1, h:h + 1],
                                              brow[0:1, 0:1], op=ALU.max), "v")

        @block.scalar
        def _(s):
            S.wait(s, "v")
            S.inc(s.activation(bcrow[:, :], kmx[:, :], AF.Ln, bias=eps30[:, :]), "s")

        @block.vector
        def _(v):
            S.wait(v, "s")
            S.inc(v.tensor_scalar(bcrow[:, :], bcrow[:, :], 0.5, 0.0,
                                  op0=ALU.mult, op1=ALU.add), "v")
        bc_done = S.c["v"]

        for tt in range(32):
            @block.tensor
            def _(t, tt=tt):
                if tt == 0:
                    S.wait(t, "v")
                else:
                    S.wait(t, "s")
                for j in range(ND):
                    t.matmul(pB[:, 0:256], xT[:, j * T + 128 * tt: j * T + 128 * (tt + 1)],
                             vwS[:, j * 256:(j + 1) * 256], start=(j == 0), stop=False)
                S.inc(t.matmul(pB[:, 0:256], onesrow[0:1, :], vbrow[0:1, :],
                               start=False, stop=True), "t")

            @block.scalar
            def _(s, tt=tt):
                S.wait(s, "t")
                for h in range(NKV):
                    ms = s.copy(vS[:, tt * 260 + 65 * h: tt * 260 + 65 * h + 64],
                                pB[:, 64 * h:64 * (h + 1)])
                S.inc(ms, "s")

        for qb in range(8):
            qc0 = 512 * qb

            qp_s0 = S.c["s"]
            for m in range(ND):
                @block.tensor
                def _(t, qb=qb, m=m, qp_s0=qp_s0):
                    if m == 0:
                        S.wait(t, "s")
                        S.wait(t, "v")
                    if m >= 2:
                        t.wait_ge(S.h["s"], qp_s0 + (m - 1))
                    for j in range(ND):
                        mm = t.matmul(pA[:, :] if m % 2 == 0 else pC[:, :],
                                      qwS[:, j * 1024 + 128 * m: j * 1024 + 128 * (m + 1)],
                                      xT[:, j * T + qc0: j * T + qc0 + 512],
                                      start=(j == 0), stop=(j == ND - 1))
                    S.inc(mm, "t")

                @block.scalar
                def _(s, qb=qb, m=m):
                    S.wait(s, "t")
                    S.inc(s.activation(qTq[:, 512 * m:512 * (m + 1)],
                                       pA[:, :] if m % 2 == 0 else pC[:, :],
                                       AF.Identity, bias=qbcol[:, m:m + 1]), "s")

                @block.vector
                def _(v, qb=qb, m=m):
                    S.wait(v, "s")
                    S.inc(v.tensor_tensor(sqq[:, 512 * m:512 * (m + 1)],
                                          qTq[:, 512 * m:512 * (m + 1)],
                                          qTq[:, 512 * m:512 * (m + 1)], op=ALU.mult), "v")

            for h in range(HEADS):
                jt, po = h // 2, 64 * (h % 2)
                kv = h // 4

                @block.tensor
                def _(t, qb=qb, h=h, jt=jt, po=po):
                    S.wait(t, "v")
                    if h > 0:
                        S.wait(t, "s")
                    S.inc(t.matmul(pR8[0:1, 0:512], ones64[:, (h % 2):(h % 2) + 1],
                                   sqq[:, 512 * jt:512 * (jt + 1)], start=True, stop=True), "t")

                @block.vector
                def _(v, qb=qb, h=h, jt=jt, po=po):
                    S.wait(v, "t")
                    S.inc(v.tensor_copy(q65[0:64, :], qTq[po:po + 64, 512 * jt:512 * (jt + 1)]), "v")

                @block.scalar
                def _(s, qb=qb, h=h, kv=kv):
                    S.wait(s, "v")
                    s.activation(brow[0:1, :], pR8[0:1, 0:512], AF.Ln, bias=eps30[:, :])
                    s.activation(q65[64:65, :], brow[0:1, :], AF.Exp, scale=0.5,
                                 bias=bcrow[0:1, kv:kv + 1])
                    S.inc(s.activation(q65[64:65, :], q65[64:65, :], AF.Identity, scale=-1.0), "s")
                q65_s = S.c["s"]

                for kt in range(32):
                    @block.tensor
                    def _(t, qb=qb, h=h, kv=kv, kt=kt, q65_s=q65_s):
                        if kt == 0:
                            t.wait_ge(S.h["s"], q65_s)
                        if kt >= 2:
                            S.wait(t, "s", S.c["s"] - 1)
                        p = pA if kt % 2 == 0 else pC
                        S.inc(t.matmul(p[:, :], K65[kv][0:65, 128 * kt:128 * (kt + 1)],
                                       q65[0:65, :], start=True, stop=True), "t")

                    @block.scalar
                    def _(s, qb=qb, h=h, kt=kt):
                        S.wait(s, "t")
                        p = pA if kt % 2 == 0 else pC
                        S.inc(s.activation(et0[:, :] if kt % 2 == 0 else et1[:, :],
                                           p[:, :], AF.Exp, scale=0.125), "s")

                    @block.tensor
                    def _(t, qb=qb, h=h, kv=kv, kt=kt):
                        S.wait(t, "s")
                        S.inc(t.matmul(pO[0:65, :], vS[:, kt * 260 + 65 * kv: kt * 260 + 65 * (kv + 1)],
                                       et0[:, :] if kt % 2 == 0 else et1[:, :],
                                       start=(kt == 0), stop=(kt == 31)), "t")

                @block.scalar
                def _(s, qb=qb, h=h, jt=jt, po=po):
                    S.wait(s, "t")
                    s.activation(rdrow[0:1, :], pO[64:65, :], AF.Ln)
                    s.activation(rdrow[0:1, :], rdrow[0:1, :], AF.Exp, scale=-1.0)
                    S.inc(s.copy(rdrow16[0:1, :], rdrow[0:1, :]), "s")

                @block.tensor
                def _(t, qb=qb, h=h):
                    S.wait(t, "s")
                    S.inc(t.matmul(pR8[:, 0:512], onesrow[0:1, :], rdrow16[0:1, :],
                                   start=True, stop=True), "t")

                @block.scalar
                def _(s, qb=qb, h=h, jt=jt, po=po):
                    S.wait(s, "t")
                    s.copy(rdbc[:, :], pR8[:, 0:512])
                    S.inc(s.copy(oTq[po:po + 64, 512 * jt:512 * (jt + 1)], pO[0:64, :]), "s")

                @block.vector
                def _(v, qb=qb, h=h, jt=jt, po=po):
                    S.wait(v, "s")
                    S.inc(v.tensor_tensor(oTq[po:po + 64, 512 * jt:512 * (jt + 1)],
                                          oTq[po:po + 64, 512 * jt:512 * (jt + 1)],
                                          rdbc[po:po + 64, :], op=ALU.mult), "v")

            ow_s0 = S.c["s"]
            for m in range(ND):
                @block.tensor
                def _(t, qb=qb, m=m, ow_s0=ow_s0):
                    S.wait(t, "v")
                    if m >= 2:
                        t.wait_ge(S.h["s"], ow_s0 + (m - 1))
                    p = pA if m % 2 == 0 else pC
                    for it in range(ND):
                        mm = t.matmul(p[:, :], owS[:, it * 1024 + 128 * m: it * 1024 + 128 * (m + 1)],
                                      oTq[:, it * 512: it * 512 + 512],
                                      start=(it == 0), stop=(it == ND - 1))
                    S.inc(mm, "t")

                @block.scalar
                def _(s, qb=qb, m=m):
                    S.wait(s, "t")
                    p = pA if m % 2 == 0 else pC
                    S.inc(s.activation(obscr[:, :], p[:, :], AF.Identity, bias=obcol[:, m:m + 1]), "s")

                @block.vector
                def _(v, qb=qb, m=m, qc0=qc0):
                    S.wait(v, "s")
                    S.inc(v.tensor_tensor(xT[:, m * T + qc0: m * T + qc0 + 512], obscr[:, :],
                                          xT[:, m * T + qc0: m * T + qc0 + 512], op=ALU.add), "v")

        if stage <= 1:
            @block.gpsimd
            def _(g):
                S.wait(g, "v")
                S.wait(g, "s")
                for j in range(ND):
                    S.inc(g.dma_start(out=out_d[128 * j:128 * (j + 1), :], in_=xT[:, j * T:(j + 1) * T]), "st", 16)
                S.wait(g, "st")
            return nc
        for grp in range(16):
            g0 = 256 * grp

            @block.vector
            def _(v, grp=grp, g0=g0):
                S.wait(v, "v", S.c["v"])
                if grp == 0:
                    S.wait(v, "s")
                S.wait(v, "st")
                for j in range(ND):
                    mv = v.tensor_copy(sres[:, j * 256:(j + 1) * 256], xT[:, j * T + g0: j * T + g0 + 256])
                S.inc(mv, "v")
            emit_rms_d(nc, block, S, sres, 256, lnenv, epsr, n1col)

            @block.vector
            def _(v, grp=grp, g0=g0):
                S.wait(v, "v", S.c["v"])
                for j in range(ND):
                    mv = v.tensor_copy(xT[:, j * T + g0: j * T + g0 + 256], sres[:, j * 256:(j + 1) * 256])
                S.inc(mv, "v")

        if stage <= 2:
            @block.gpsimd
            def _(g):
                S.wait(g, "v")
                S.wait(g, "s")
                for j in range(ND):
                    S.inc(g.dma_start(out=out_d[128 * j:128 * (j + 1), :], in_=xT[:, j * T:(j + 1) * T]), "st", 16)
                S.wait(g, "st")
            return nc
        vL0 = S.c["v"]
        for tt in range(32):
            @block.tensor
            def _(t, tt=tt):
                if tt < 2:
                    S.wait(t, "v")
                else:
                    S.wait(t, "v", vL0 + tt - 1)
                p = pA if tt % 2 == 0 else pC
                for j in range(ND):
                    mm = t.matmul(p[0:8, 0:128],
                                  gwS[:, j * 8:(j + 1) * 8],
                                  xT[:, j * T + 128 * tt: j * T + 128 * (tt + 1)],
                                  start=(j == 0), stop=(j == ND - 1))
                S.inc(mm, "t")

            @block.vector
            def _(v, tt=tt):
                S.wait(v, "t")
                p = pA if tt % 2 == 0 else pC
                for c in range(4):
                    mv = v.transpose(lgB[32 * c:32 * (c + 1), 8 * tt:8 * tt + 32],
                                     p[0:32, 32 * c:32 * (c + 1)])
                S.inc(mv, "v")

        def top1_grp(v, src_of, dst_of, mvbase):
            for k in range(4):
                s8 = src_of(k)
                v.tensor_tensor(mxG[:, 8 * k:8 * k + 4], s8[:, 0:4], s8[:, 4:8], op=ALU.max)
            for k in range(4):
                v.tensor_tensor(m1G[:, 4 * k:4 * k + 2], mxG[:, 8 * k:8 * k + 2],
                                mxG[:, 8 * k + 2:8 * k + 4], op=ALU.max)
            for k in range(4):
                v.tensor_tensor(mvalD[:, mvbase + k:mvbase + k + 1], m1G[:, 4 * k:4 * k + 1],
                                m1G[:, 4 * k + 1:4 * k + 2], op=ALU.max)
            for k in range(4):
                v.tensor_tensor(hhG[:, 8 * k:8 * k + 8], src_of(k),
                                mvalD[:, mvbase + k:mvbase + k + 1].broadcast_to((128, 8)),
                                op=ALU.is_ge)
            for k in range(4):
                v.tensor_tensor(candG[:, 8 * k:8 * k + 8], hhG[:, 8 * k:8 * k + 8],
                                iotaM[:, :], op=ALU.mult)
            for k in range(4):
                v.tensor_scalar_add(candG[:, 8 * k:8 * k + 8], candG[:, 8 * k:8 * k + 8], 1000.0)
            for k in range(4):
                v.tensor_tensor(mxG[:, 8 * k:8 * k + 4], candG[:, 8 * k:8 * k + 4],
                                candG[:, 8 * k + 4:8 * k + 8], op=ALU.min)
            for k in range(4):
                v.tensor_tensor(m1G[:, 4 * k:4 * k + 2], mxG[:, 8 * k:8 * k + 2],
                                mxG[:, 8 * k + 2:8 * k + 4], op=ALU.min)
            for k in range(4):
                v.tensor_tensor(m1G[:, 4 * k + 2:4 * k + 3], m1G[:, 4 * k:4 * k + 1],
                                m1G[:, 4 * k + 1:4 * k + 2], op=ALU.min)
            for k in range(4):
                v.tensor_scalar_mul(m1G[:, 4 * k + 3:4 * k + 4], m1G[:, 4 * k + 2:4 * k + 3], -1.0)
            for k in range(4):
                v.tensor_tensor(hhG[:, 8 * k:8 * k + 8], iotaP[:, :],
                                m1G[:, 4 * k + 2:4 * k + 3].broadcast_to((128, 8)), op=ALU.is_ge)
            for k in range(4):
                v.tensor_tensor(candG[:, 8 * k:8 * k + 8], iotaN[:, :],
                                m1G[:, 4 * k + 3:4 * k + 4].broadcast_to((128, 8)), op=ALU.is_ge)
            mv = None
            for k in range(4):
                mv = v.tensor_tensor(dst_of(k), hhG[:, 8 * k:8 * k + 8],
                                     candG[:, 8 * k:8 * k + 8], op=ALU.mult)
            return mv

        for g8 in range(8):
            @block.vector
            def _(v, g8=g8):
                t0 = 4 * g8
                src0 = lambda k: lgB[:, 8 * (t0 + k):8 * (t0 + k) + 8]
                top1_grp(v, src0, lambda k: a0b[:, 8 * (t0 + k):8 * (t0 + k) + 8], t0)
                for k in range(4):
                    v.scalar_tensor_tensor(mskG[:, 8 * k:8 * k + 8],
                                           a0b[:, 8 * (t0 + k):8 * (t0 + k) + 8],
                                           bigneg[:, 0:1], src0(k), op0=ALU.mult, op1=ALU.add)
                mv = top1_grp(v, lambda k: mskG[:, 8 * k:8 * k + 8],
                              lambda k: a1b[:, 8 * (t0 + k):8 * (t0 + k) + 8], 32 + t0)
                if g8 == 7:
                    S.inc(mv, "v")

        @block.scalar
        def _(s):
            S.wait(s, "v")
            S.inc(s.activation(pb64[:, 0:64], mvalD[:, 0:64], AF.Sigmoid), "s")

        @block.vector
        def _(v):
            S.wait(v, "s")
            S.inc(v.tensor_tensor(mxG[:, 0:32], pb64[:, 0:32], pb64[:, 32:64], op=ALU.add), "v")

        @block.scalar
        def _(s):
            S.wait(s, "v")
            s.activation(mxG[:, 0:32], mxG[:, 0:32], AF.Ln, bias=epswc[:, :])
            S.inc(s.activation(candG[:, 0:32], mxG[:, 0:32], AF.Exp, scale=-1.0), "s")

        @block.vector
        def _(v):
            S.wait(v, "s")
            v.tensor_tensor(gw01[:, 0:32], pb64[:, 0:32], candG[:, 0:32], op=ALU.mult)
            v.tensor_tensor(gw01[:, 32:64], pb64[:, 32:64], candG[:, 0:32], op=ALU.mult)
            S.inc(v.tensor_tensor(cnt16[:, 0:256], a0b[:, 0:256], a1b[:, 0:256], op=ALU.add), "v")

        @block.tensor
        def _(t):
            S.wait(t, "v")
            for tt in range(32):
                mm = t.matmul(pR8[0:1, 8:16], onescol[:, :], cnt16[:, 8 * tt:8 * (tt + 1)],
                              start=(tt == 0), stop=(tt == 31))
            S.inc(mm, "t")

        @block.vector
        def _(v):
            S.wait(v, "t")
            S.inc(v.tensor_tensor(runrow[0:1, :], pR8[0:1, 8:16], zrow[0:1, :], op=ALU.add), "v")

        @block.gpsimd
        def _(g):
            S.wait(g, "v")
            S.inc(g.dma_start(out=cnt_in.ap(), in_=runrow[0:1, :]), "dmb", 16)
            S.wait(g, "dmb")
            S.inc(g.collective_compute("AllGather", ALU.bypass, replica_groups=[list(range(8))],
                                       ins=[cnt_in.ap().opt()], outs=[cnt_out.ap().opt()]), "cc")
            g.wait_ge(S.h["cc"], 1)
            S.inc(g.dma_start(out=cnt32[0:8, 0:8], in_=cnt_out.ap()), "dmb", 16)

        @block.scalar
        def _(s):
            S.wait(s, "dmb")
            s.copy(cnthi[0:8, 0:8], cnt32[0:8, 0:8])
            S.inc(s.copy(cm16[0:8, 0:1], cmaskS[0:8, 0:1]), "s")

        @block.vector
        def _(v):
            S.wait(v, "s")
            S.inc(v.tensor_tensor(cntlo[0:8, 0:8], cnt32[0:8, 0:8], cnthi[0:8, 0:8],
                                  op=ALU.subtract), "v")

        @block.tensor
        def _(t):
            S.wait(t, "v")
            t.matmul(pR8[0:1, 24:32], cm16[0:8, 0:1], cnthi[0:8, 0:8], start=True, stop=False)
            S.inc(t.matmul(pR8[0:1, 24:32], cm16[0:8, 0:1], cntlo[0:8, 0:8],
                           start=False, stop=True), "t")

        @block.vector
        def _(v):
            S.wait(v, "t")
            S.inc(v.tensor_tensor(runrow[0:1, :], pR8[0:1, 24:32], zrow[0:1, :], op=ALU.add), "v")

        for tt in range(32):
            @block.scalar
            def _(s, tt=tt):
                S.wait(s, "v")
                S.inc(s.copy(runhi[0:1, :], runrow[0:1, :]), "s")

            @block.vector
            def _(v, tt=tt):
                S.wait(v, "s")
                S.inc(v.tensor_tensor(runlo16[0:1, :], runrow[0:1, :], runhi[0:1, :],
                                      op=ALU.subtract), "v")

            @block.tensor
            def _(t, tt=tt):
                S.wait(t, "v")
                t.matmul(pR8[0:128, 16:24], triS[:, :], cnt16[:, 8 * tt:8 * (tt + 1)],
                         start=True, stop=False)
                t.matmul(pR8[0:128, 16:24], onesrow[0:1, :], runhi[0:1, :], start=False, stop=False)
                t.matmul(pR8[0:128, 16:24], onesrow[0:1, :], runlo16[0:1, :], start=False, stop=True)
                S.inc(t.matmul(pR8[0:1, 24:32], onescol[:, :], cnt16[:, 8 * tt:8 * (tt + 1)],
                               start=True, stop=True), "t")

            @block.vector
            def _(v, tt=tt):
                S.wait(v, "t")
                o8 = slice(8 * tt, 8 * (tt + 1))
                v.tensor_tensor(exlf[:, :], pR8[:, 16:24], zero8[:, :], op=ALU.add)
                v.tensor_tensor(runrow[0:1, :], runrow[0:1, :], pR8[0:1, 24:32], op=ALU.add)
                v.tensor_tensor(mskG[:, 0:8], a0b[:, o8], exlf[:, :], op=ALU.mult)
                v.tensor_tensor(mskG[:, 8:16], a1b[:, o8], exlf[:, :], op=ALU.mult)
                v.tensor_scalar_mul(mskG[:, 16:24], a1b[:, o8], gw01[:, 32 + tt:33 + tt])
                v.tensor_scalar_mul(mskG[:, 24:32], a0b[:, o8], gw01[:, tt:tt + 1])
                v.tensor_reduce(rnk[:, 0:1], mskG[:, 0:8], axis=AX.X, op=ALU.add)
                S.inc(v.tensor_reduce(rnk[:, 1:2], mskG[:, 8:16], axis=AX.X, op=ALU.add), "v")

            @block.scalar
            def _(s, tt=tt):
                S.wait(s, "v")
                S.inc(s.copy(rnk[:, 4:6], rnk[:, 0:2]), "s")

            @block.vector
            def _(v, tt=tt):
                S.wait(v, "s")
                v.tensor_tensor(rnk[:, 2:3], capm1col[:, 0:1], rnk[:, 4:5], op=ALU.is_ge)
                S.inc(v.tensor_tensor(rnk[:, 3:4], capm1col[:, 0:1], rnk[:, 5:6], op=ALU.is_ge), "v")

            @block.scalar
            def _(s, tt=tt):
                S.wait(s, "v")
                S.inc(s.copy(rnk[:, 6:8], rnk[:, 2:4]), "s")

            @block.vector
            def _(v, tt=tt):
                S.wait(v, "s")
                o8 = slice(8 * tt, 8 * (tt + 1))
                v.tensor_scalar_mul(candG[:, 0:8], mskG[:, 24:32], rnk[:, 6:7])
                v.tensor_scalar_mul(candG[:, 8:16], mskG[:, 16:24], rnk[:, 7:8])
                v.tensor_tensor(hhG[:, 0:8], zero8[:, :], zero8[:, :], op=ALU.max)
                v.tensor_tensor(hhG[:, 8:16], zero8[:, :], zero8[:, :], op=ALU.max)
                S.inc(v.tensor_tensor(ge16[:, o8], candG[:, 0:8], candG[:, 8:16], op=ALU.add), "v")

            @block.tensor
            def _(t, tt=tt):
                S.wait(t, "v")
                S.inc(t.transpose(pT0[0:8, 128:256], ge16[:, 8 * tt:8 * (tt + 1)], ident[:, :]), "t")

            @block.vector
            def _(v, tt=tt):
                S.wait(v, "t")
                S.inc(v.tensor_copy(geT[0:8, 128 * tt:128 * (tt + 1)], pT0[0:8, 128:256]), "v")

        if stage == 3:
            @block.gpsimd
            def _(g):
                S.wait(g, "v")
                S.wait(g, "t")
                S.inc(g.dma_start(out=out_d[0:128, 0:256], in_=a0b[:, :]), "st", 16)
                S.inc(g.dma_start(out=out_d[0:128, 256:512], in_=a1b[:, :]), "st", 16)
                S.inc(g.dma_start(out=out_d[0:128, 512:576], in_=gw01[:, :]), "st", 16)
                S.inc(g.dma_start(out=out_d[0:128, 576:640], in_=pb64[:, :]), "st", 16)
                S.inc(g.dma_start(out=out_d[0:1, 832:840], in_=runrow[0:1, :]), "st", 16)
                S.inc(g.dma_start(out=out_d[0:8, 1024:1032], in_=cnt32[0:8, 0:8]), "st", 16)
                S.wait(g, "st")
            return nc
        MCD = 1024
        geTs = geT
        for mcc in range(4):
            mc0 = MCD * mcc

            @block.vector
            def _(v, mcc=mcc):
                S.wait(v, "v", S.c["v"])
                S.wait(v, "t")
                S.wait(v, "st")
                S.inc(v.memset(accM[:, :], 0.0), "v")
            acc0_v = S.c["v"]

            for e in range(NEXP):
                @block.gpsimd
                def _(g, e=e, mcc=mcc, acc0_v=acc0_v):
                    if e == 0 and mcc == 0:
                        g.wait_ge(S.h["v"], acc0_v)
                    else:
                        S.wait(g, "s")
                    for ht in range(16):
                        S.inc(g.dma_start(out=w2S[:, ht * 1024:(ht + 1) * 1024],
                                          in_=w2_d[e, 128 * ht:128 * (ht + 1), :]), "dmb", 16)
                w_dmb = S.c["dmb"]

                for sub in range(MCD // 512):
                    sc0 = mc0 + 512 * sub

                    @block.tensor
                    def _(t, e=e, sub=sub, sc0=sc0, w_dmb=w_dmb):
                        if sub == 0:
                            t.wait_ge(S.h["dmb"], w_dmb)
                        S.wait(t, "v")
                        for q4 in range(4):
                            tt = sc0 // 128 + q4
                            mm = t.matmul(pBc[:, 128 * q4:128 * (q4 + 1)],
                                          rows8[:, 128 * e:128 * (e + 1)],
                                          geTs[0:8, 128 * tt:128 * (tt + 1)],
                                          start=True, stop=True)
                        S.inc(mm, "t")

                    @block.vector
                    def _(v, e=e, sub=sub):
                        S.wait(v, "t")
                        S.inc(v.tensor_copy(gbcf[:, :], pBc[:, 0:512]), "v")

                    for ht in range(16):
                        @block.gpsimd
                        def _(g, e=e, ht=ht, sub=sub):
                            wt = w1t0 if ht % 2 == 0 else w1t1
                            if not (ht < 2 and sub == 0 and e == 0 and mcc == 0):
                                S.wait(g, "t", S.c["t"] - (1 if ht >= 2 or sub > 0 or e > 0 or mcc > 0 else 0))
                            S.inc(g.dma_start(out=wt[:, 0:ND * 128].rearrange("p (j c) -> p j c", c=128),
                                              in_=w1_d[e, :, 128 * ht:128 * (ht + 1)]
                                              .rearrange("(j p) c -> p j c", p=128)), "dmb", 16)

                        @block.tensor
                        def _(t, e=e, ht=ht, sub=sub, sc0=sc0):
                            S.wait(t, "dmb")
                            if ht == 0:
                                S.wait(t, "v")
                            if ht >= 2:
                                S.wait(t, "s", S.c["s"] - 1)
                            wt = w1t0 if ht % 2 == 0 else w1t1
                            p = pA if ht % 2 == 0 else pC
                            for j in range(ND):
                                mm = t.matmul(p[:, :], wt[:, 128 * j:128 * (j + 1)],
                                              xT[:, j * T + sc0: j * T + sc0 + 512],
                                              start=(j == 0), stop=(j == ND - 1))
                            S.inc(mm, "t")

                        @block.scalar
                        def _(s, e=e, ht=ht, sub=sub):
                            S.wait(s, "t")
                            p = pA if ht % 2 == 0 else pC
                            S.inc(s.activation(HM[:, ht * 512:(ht + 1) * 512], p[:, :], AF.Gelu), "s")

                    for m in range(ND):
                        @block.tensor
                        def _(t, e=e, m=m, sub=sub):
                            if m == 0:
                                S.wait(t, "s")
                            if m >= 2:
                                S.wait(t, "v", S.c["v"] - 1)
                            p = pA if m % 2 == 0 else pC
                            for ht in range(16):
                                mm = t.matmul(p[:, :], w2S[:, ht * 1024 + 128 * m: ht * 1024 + 128 * (m + 1)],
                                              HM[:, ht * 512:(ht + 1) * 512],
                                              start=(ht == 0), stop=(ht == 15))
                            S.inc(mm, "t")

                        @block.vector
                        def _(v, e=e, m=m, sub=sub, mc0=mc0):
                            S.wait(v, "t")
                            p = pA if m % 2 == 0 else pC
                            v.tensor_tensor(yscr[:, :], p[:, :], gbcf[:, :], op=ALU.mult)
                            S.inc(v.tensor_tensor(accM[:, m * MCD + 512 * sub: m * MCD + 512 * (sub + 1)],
                                                  accM[:, m * MCD + 512 * sub: m * MCD + 512 * (sub + 1)],
                                                  yscr[:, :], op=ALU.add), "v")

                    @block.scalar
                    def _(s, e=e, sub=sub):
                        S.wait(s, "v")
                        S.inc(s.copy(rdrow16[0:1, 0:1], rdrow16[0:1, 0:1]), "s")

            for grp in range(MCD // 256):
                g0 = mc0 + 256 * grp

                @block.gpsimd
                def _(g, mcc=mcc, grp=grp, g0=g0):
                    S.wait(g, "v")
                    for j in range(ND):
                        S.inc(g.dma_start(out=xinb2[:, (grp % 2) * 2048 + j * 256:(grp % 2) * 2048 + (j + 1) * 256],
                                          in_=x_d[128 * j:128 * (j + 1), g0:g0 + 256]), "dmb", 16)

                @block.vector
                def _(v, mcc=mcc, grp=grp, g0=g0):
                    S.wait(v, "v", S.c["v"])
                    S.wait(v, "st")
                    for j in range(ND):
                        mv = v.tensor_tensor(sres[:, j * 256:(j + 1) * 256],
                                             xT[:, j * T + g0: j * T + g0 + 256],
                                             accM[:, j * MCD + 256 * grp: j * MCD + 256 * (grp + 1)],
                                             op=ALU.add)
                    S.inc(mv, "v")
                emit_rms_d(nc, block, S, sres, 256, lnenv, epsr, n2col)

                @block.vector
                def _(v, mcc=mcc, grp=grp, g0=g0):
                    S.wait(v, "v", S.c["v"])
                    S.wait(v, "dmb")
                    for j in range(ND):
                        mv = v.tensor_tensor(sres[:, j * 256:(j + 1) * 256],
                                             sres[:, j * 256:(j + 1) * 256],
                                             xinb2[:, (grp % 2) * 2048 + j * 256:(grp % 2) * 2048 + (j + 1) * 256],
                                             op=ALU.add)
                    S.inc(mv, "v")
                emit_ln_c(nc, block, S, sres, 256,
                          dict(sqg=sqg, onescol=onescol, onesrow=onesrow, statrow=statrow,
                               muhi=muhi, mulo=mulo, mulo16=mulo16, mubcf=mubcf, rstdf=rstdf,
                               lnscr=lnscr, epsc=epsc, pStat=pStat, pBc=pBc))

                @block.gpsimd
                def _(g, grp=grp, g0=g0):
                    S.wait(g, "v")
                    for j in range(ND):
                        S.inc(g.dma_start(out=out_d[128 * j:128 * (j + 1), g0:g0 + 256],
                                          in_=sres[:, j * 256:(j + 1) * 256]), "st", 16)

        @block.gpsimd
        def _(g):
            S.wait(g, "st")
    return nc


def emit_rms_d(nc, block, S, sres, W, env, epsr, gaincol):
    sqg = env["sqg"]; onescol = env["onescol"]; onesrow = env["onesrow"]
    statrow = env["statrow"]; muhi = env["muhi"]; mulo = env["mulo"]; mulo16 = env["mulo16"]
    rstdf = env["rstdf"]; lnscr = env["lnscr"]
    pStat = env["pStat"]; pBc = env["pBc"]

    @block.scalar
    def _(s):
        S.wait(s, "v")
        for j in range(ND):
            ms = s.activation(sqg[:, j * W:(j + 1) * W], sres[:, j * W:(j + 1) * W], AF.Square)
        S.inc(ms, "s")

    @block.tensor
    def _(t):
        S.wait(t, "s")
        for j in range(ND):
            mm = t.matmul(pStat[0:1, 0:W], onescol[:, :], sqg[:, j * W:(j + 1) * W],
                          start=(j == 0), stop=(j == ND - 1))
        S.inc(mm, "t")

    @block.scalar
    def _(s):
        S.wait(s, "t")
        s.activation(statrow[0:1, 0:W], pStat[0:1, 0:W], AF.Copy, scale=1.0 / D)
        s.activation(statrow[0:1, 0:W], statrow[0:1, 0:W], AF.Ln, bias=epsr[:, :])
        S.inc(s.activation(statrow[0:1, 0:W], statrow[0:1, 0:W], AF.Exp, scale=-0.5), "s")

    @block.vector
    def _(v):
        S.wait(v, "s")
        v.tensor_copy(muhi[0:1, 0:W], statrow[0:1, 0:W])
        v.tensor_tensor(mulo[0:1, 0:W], statrow[0:1, 0:W], muhi[0:1, 0:W], op=ALU.subtract)
        S.inc(v.tensor_copy(mulo16[0:1, 0:W], mulo[0:1, 0:W]), "v")

    @block.tensor
    def _(t):
        S.wait(t, "v")
        t.matmul(pBc[:, 0:W], onesrow[0:1, :], muhi[0:1, 0:W], start=True, stop=False)
        S.inc(t.matmul(pBc[:, 0:W], onesrow[0:1, :], mulo16[0:1, 0:W], start=False, stop=True), "t")

    @block.vector
    def _(v):
        S.wait(v, "t")
        v.tensor_copy(rstdf[:, 0:W], pBc[:, 0:W])
        for j in range(ND):
            v.tensor_tensor(lnscr[:, 0:W], sres[:, j * W:(j + 1) * W], rstdf[:, 0:W], op=ALU.mult)
            mv = v.tensor_scalar(sres[:, j * W:(j + 1) * W], lnscr[:, 0:W],
                                 gaincol[:, j:j + 1], 0.0, op0=ALU.mult, op1=ALU.add)
        S.inc(mv, "v")
